# version 17
# baseline (speedup 1.0000x reference)
import numpy as np
import ml_dtypes
import concourse.bass as bass
import concourse.mybir as mybir
from concourse.bass_utils import run_bass_kernel_spmd

F32 = mybir.dt.float32
BF16 = mybir.dt.bfloat16
AF = mybir.ActivationFunctionType
ALU = mybir.AluOpType
AX = mybir.AxisListType

D = 2048
NH = 16
HD = 128
HID = 8192
NHT = HID // 128
EPS = 1e-5
ISQ = float(1.0 / np.sqrt(128.0))

ENGINES = ("tensor", "scalar", "vector", "sync", "gpsimd")


class Sched:

    def __init__(self, nc):
        self.nc = nc
        self.items = {e: [] for e in ENGINES}
        self.sems = {}
        self.counts = {}

    def _sem(self, name):
        if name not in self.sems:
            self.sems[name] = self.nc.alloc_semaphore(name=name)
            self.counts[name] = 0
        return self.sems[name]

    def emit(self, eng, fn, waits=(), inc=None, inc_by=1):
        ev = None
        if inc is not None:
            self._sem(inc)
            self.counts[inc] += inc_by
            ev = (inc, self.counts[inc])
        ws = tuple((s, v) for (s, v) in waits if s is not None and v > 0)
        for s, _ in ws:
            self._sem(s)
        self.items[eng].append((fn, ws, inc, inc_by))
        return ev

    def replay(self, block):
        for ename in ENGINES:
            seq = self.items[ename]
            if not seq:
                continue

            def mk(seq):
                def body(eng):
                    wm = {}
                    for fn, waits, inc, inc_by in seq:
                        for (sname, val) in waits:
                            if wm.get(sname, 0) >= val:
                                continue
                            eng.wait_ge(self.sems[sname], val)
                            wm[sname] = val
                        ins = fn(eng)
                        if inc is not None:
                            ins.then_inc(self.sems[inc], inc_by)
                return body
            getattr(block, ename)(mk(seq))


def _later(*evs):
    best = {}

    def add(ev):
        if ev is None:
            return
        if isinstance(ev, tuple) and len(ev) == 2 and isinstance(ev[0], str):
            s, v = ev
            if best.get(s, -1) < v:
                best[s] = v
        else:
            for e in ev:
                add(e)
    for ev in evs:
        add(ev)
    return tuple(best.items())


def build_program(T):
    TT = T // 128
    KT = T // 128
    NP = KT // 2
    NCH = T // 512
    assert TT % 4 == 0

    nc = bass.Bass("TRN2", target_bir_lowering=False)

    x_d = nc.dram_tensor("x", [T, D], F32, kind="ExternalInput")
    wqkvT_d = nc.dram_tensor("wqkvT", [D, 2304], F32, kind="ExternalInput")
    woT_d = nc.dram_tensor("woT", [D, D], F32, kind="ExternalInput")
    p1blk_d = nc.dram_tensor("p1blk", [NHT, 128, 2048], F32,
                             kind="ExternalInput")
    p2T_d = nc.dram_tensor("p2T", [HID, D], F32, kind="ExternalInput")
    b1r_d = nc.dram_tensor("b1r", [128, NHT], F32, kind="ExternalInput")
    b2row_d = nc.dram_tensor("b2row", [1, D], F32, kind="ExternalInput")
    cosA_d = nc.dram_tensor("cosA", [128, TT * 64], F32, kind="ExternalInput")
    sinA_d = nc.dram_tensor("sinA", [128, TT * 64], F32, kind="ExternalInput")
    ident_d = nc.dram_tensor("ident", [128, 128], BF16, kind="ExternalInput")
    onescol_d = nc.dram_tensor("onescol", [128, 1], F32, kind="ExternalInput")
    onesr_d = nc.dram_tensor("onesr", [1, 128], F32, kind="ExternalInput")
    ones128_d = nc.dram_tensor("ones128", [128, 128], F32,
                               kind="ExternalInput")
    y_d = nc.dram_tensor("y", [T, D], F32, kind="ExternalOutput")
    woT_bf = nc.dram_tensor("woT_bf", [D, D], BF16, kind="Internal")
    p1_bf = nc.dram_tensor("p1_bf", [NHT, 128, 2048], BF16, kind="Internal")
    p2_bf = nc.dram_tensor("p2_bf", [HID, D], BF16, kind="Internal")
    q_scr = nc.dram_tensor("q_scr", [NH, 128, T], BF16, kind="Internal")
    x1_scr = nc.dram_tensor("x1_scr", [T, D], F32, kind="Internal")

    ORIGIN = 20480
    LIMIT = 229376 - 128
    off = [ORIGIN]

    def sb(name, cols, dt, at=None, parts=128):
        nbytes = cols * (2 if dt == BF16 else 4)
        if at is None:
            at = off[0]
            off[0] = at + ((nbytes + 31) // 32) * 32
        return nc.alloc_sbuf_tensor_at(name, [parts, cols], dt, offset=at)

    K_sb = sb("K_sb", T, BF16)
    V_sb = sb("V_sb", T, BF16)
    x_sb = [sb("x0", D, F32), sb("x1b", D, F32)]
    stats = sb("stats", 32, F32)
    ident_sb = sb("ident_s", 128, BF16)
    onescol_sb = sb("onescol_s", 1, F32)
    b1r_sb = sb("b1r_s", NHT, F32)
    onesr_sb = sb("onesr_s", 128, F32, parts=1)
    ones128_sb = sb("ones128_s", 128, F32)

    AR0 = off[0]
    E_sb = sb("E_sb", 8 * 1024, BF16, at=AR0)
    y_stage = sb("y_stage", 2 * 2048, F32, at=AR0)
    aT_sb = sb("aT_sb", NH * 512, BF16, at=AR0 + 16384)
    Qs = [sb("Qs0", 512, BF16, at=AR0 + 32768),
          sb("Qs1", 512, BF16, at=AR0 + 33792)]
    hT_sb = sb("hT", NHT * 512, BF16, at=AR0 + 34816)
    stg = sb("stg", 16 * 512, BF16, at=AR0 + 100352)
    ln2b4 = sb("ln2b4", 2 * 2048, BF16, at=AR0 + 116736)
    avs = sb("avs", 512, F32, at=AR0 + 116736)
    p1s = [sb("p1a", 2048, BF16, at=AR0 + 124928),
           sb("p1b", 2048, BF16, at=AR0 + 129024)]
    wr = [sb("wr0", 2048, BF16, at=AR0 + 133120),
          sb("wr1", 2048, BF16, at=AR0 + 137216)]
    x1t = [sb("x1t0", 2048, F32, at=AR0 + 141312),
           sb("x1t1", 2048, F32, at=AR0 + 149504)]
    b2b_sb = sb("b2b", 2048, F32, at=AR0 + 157696)
    SS = [sb("S_sb", 512, F32, at=AR0 + 165888),
          sb("S2_sb", 512, F32, at=AR0 + 167936)]
    tmpP = sb("tmpP", 512, F32, at=AR0 + 169984)
    rD_sb = sb("rD_sb", 512, F32, at=AR0 + 172032)
    CH_END = AR0 + 174080
    wA_sb = sb("wA", 16 * 2304, BF16, at=AR0)
    cos_sb = sb("cosA_s", TT * 64, F32, at=AR0 + 73728)
    sin_sb = sb("sinA_s", TT * 64, F32, at=AR0 + 81920)
    rt = [sb(f"rt{i}", 1088, BF16, at=AR0 + 90112 + i * 2176)
          for i in range(4)]
    qk_sb = sb("qk", 2176, BF16, at=AR0 + 98816)
    lnT_sb = sb("lnT", D, BF16, at=AR0 + 103168)
    ln_bf = sb("ln_bf", D, BF16, at=AR0 + 107264)
    qstg = sb("qstg", 2 * 8192, BF16, at=AR0 + 111360)
    atmpA = sb("atmpA", D, BF16, at=AR0 + 144128)
    xA = [x_sb[0], x_sb[1],
          sb("xA2", D, F32, at=AR0 + 148224),
          sb("xA3", D, F32, at=AR0 + 156416)]
    qk2_sb = sb("qk2", 2176, BF16, at=AR0 + 164608)
    A_END = AR0 + 168960
    assert max(CH_END, A_END) <= LIMIT, f"sbuf overflow {CH_END} > {LIMIT}"

    PA = nc.alloc_psum_tensor("PA", [128, 2048], F32)
    PB = nc.alloc_psum_tensor("PB", [128, 2048], F32)
    PP = [PA[:, 0:1024], PA[:, 1024:2048]]
    AVp = PB[:, 0:512]
    M1 = [PB[:, 512:1024], PB[:, 1024:1536]]
    B7 = PB[:, 1536:2048]
    SEG = PA[:, 0:1024]
    TRP = PB[:, 0:1024].bitcast(BF16)
    TRP2 = PB[:, 1024:2048].bitcast(BF16)
    TRC = [PA[:, 0:1024].bitcast(BF16),
           PB[:, 0:1024].bitcast(BF16)]

    S = Sched(nc)

    def mm(ps, lhsT, rhs, start, stop, waits=(), inc=None):
        return S.emit("tensor",
                      lambda e, o=ps, l=lhsT, r=rhs: e.matmul(
                          o, l, r, start=start, stop=stop),
                      waits=waits, inc=inc)

    def mm_sc(ps, lhsT, rhs, start, stop, waits=(), inc=None):
        return S.emit("tensor", lambda e, o=ps, l=lhsT, r=rhs: e.matmul(
            o, l, r, start=start, stop=stop), waits=waits, inc=inc)

    def mm_avv(ps, lhsT, rhs, start, stop, waits=(), inc=None):
        return S.emit("tensor", lambda e, o=ps, l=lhsT, r=rhs: e.matmul(
            o, l, r, start=start, stop=stop), waits=waits, inc=inc)

    def mm_f1(ps, lhsT, rhs, start, stop, waits=(), inc=None):
        return S.emit("tensor", lambda e, o=ps, l=lhsT, r=rhs: e.matmul(
            o, l, r, start=start, stop=stop), waits=waits, inc=inc)

    def mm_f2(ps, lhsT, rhs, start, stop, waits=(), inc=None):
        return S.emit("tensor", lambda e, o=ps, l=lhsT, r=rhs: e.matmul(
            o, l, r, start=start, stop=stop), waits=waits, inc=inc)

    def mm_op(ps, lhsT, rhs, start, stop, waits=(), inc=None):
        return S.emit("tensor", lambda e, o=ps, l=lhsT, r=rhs: e.matmul(
            o, l, r, start=start, stop=stop), waits=waits, inc=inc)

    def tr(ps, in_, waits=(), inc=None):
        return S.emit("tensor",
                      lambda e, o=ps, i=in_: e.transpose(o, i, ident_sb[:, :]),
                      waits=waits, inc=inc)

    def tr_ln(ps, in_, waits=(), inc=None):
        return S.emit("tensor",
                      lambda e, o=ps, i=in_: e.transpose(o, i, ident_sb[:, :]),
                      waits=waits, inc=inc)

    def tr_qk(ps, in_, waits=(), inc=None):
        return S.emit("tensor",
                      lambda e, o=ps, i=in_: e.transpose(o, i, ident_sb[:, :]),
                      waits=waits, inc=inc)

    def dma(q, out_ap, in_ap, waits=(), sem="lds", serial=True):
        if serial:
            pre = S.counts.get(sem, 0)
            if pre > 0:
                waits = tuple(waits) + ((sem, pre),)
        return S.emit(q, lambda e, o=out_ap, i=in_ap: e.dma_start(o, i),
                      waits=waits, inc=sem, inc_by=16)

    def act(fn, waits=()):
        return S.emit("scalar", fn, waits=waits, inc="act")

    def dve(fn, waits=()):
        return S.emit("vector", fn, waits=waits, inc="dve")

    def pool(fn, waits=()):
        return S.emit("gpsimd", fn, waits=waits, inc="pool")

    ev_wA = None
    for ec in range(16):
        ev_wA = dma("gpsimd", wA_sb[:, ec * 2304:(ec + 1) * 2304],
                    wqkvT_d[ec * 128:(ec + 1) * 128, :], sem="sgA",
                    serial=False)
    cast_pieces = []
    for j in range(16):
        cast_pieces.append((woT_bf[j * 128:(j + 1) * 128, :],
                            woT_d[j * 128:(j + 1) * 128, :], "sgW"))
    for j in range(32):
        k8 = NHT // 32
        cast_pieces.append((p1_bf[j * k8:(j + 1) * k8],
                            p1blk_d[j * k8:(j + 1) * k8], "sgP1"))
    for j in range(32):
        k8 = HID // 32
        cast_pieces.append((p2_bf[j * k8:(j + 1) * k8, :],
                            p2T_d[j * k8:(j + 1) * k8, :], "sgP2"))
    cast_ev = {"sgW": None, "sgP1": None, "sgP2": None}
    cast_i = [0]

    def emit_casts(n, waits=()):
        for _ in range(n):
            if cast_i[0] >= len(cast_pieces):
                return
            dst, srcp, sem = cast_pieces[cast_i[0]]
            cast_i[0] += 1
            cast_ev[sem] = dma("gpsimd", dst, srcp, waits=waits, sem=sem,
                               serial=False)

    ev_ini = None
    for dst, src in [(ident_sb[:, :], ident_d[:, :]),
                     (onescol_sb[:, :], onescol_d[:, :]),
                     (onesr_sb[:, :], onesr_d[:, :]),
                     (ones128_sb[:, :], ones128_d[:, :]),
                     (b1r_sb[:, :], b1r_d[:, :]),
                     (cos_sb[:, :], cosA_d[:, :]),
                     (sin_sb[:, :], sinA_d[:, :])]:
        ev_ini = dma("sync", dst, src, sem="sini", serial=False)
    ev_eps = dve(lambda e: e.memset(stats[:, 7:8], EPS))
    ev_invd = dve(lambda e: e.memset(stats[:, 15:16], 1.0 / D))

    ldx = {}
    ev_lnap = {}
    ev_red = {}
    ev_half = {}
    ev_P1ev = {}
    ev_qfl = {}
    ev_vc = {}
    ev_kc = {}
    ev_trqk = {}

    for tt in range(min(2, TT)):
        ldx[tt] = dma("sync", x_sb[tt % 2][:, :],
                      x_d[tt * 128:(tt + 1) * 128, :], sem=f"sx{tt % 2}")

    def ln_chain(xap, so, w_extra, out_ap, dump_ap):
        e_sq = act(lambda e: e.activation(dump_ap, xap, AF.Square,
                                          accum_out=stats[:, so + 6:so + 7]),
                   waits=w_extra)
        e_rs = dve(lambda e: e.reduce_sum(stats[:, so + 0:so + 1], xap,
                                          axis=AX.X), waits=w_extra)
        e_mean = dve(lambda e: e.tensor_scalar(
            stats[:, so + 1:so + 2], stats[:, so + 0:so + 1], 1.0 / D, None,
            ALU.mult), waits=_later(e_rs))
        e_msq = dve(lambda e: e.tensor_scalar(
            stats[:, so + 2:so + 3], stats[:, so + 1:so + 2],
            stats[:, so + 1:so + 2], None, ALU.mult), waits=_later(e_mean))
        e_var = dve(lambda e: e.scalar_tensor_tensor(
            stats[:, so + 3:so + 4], stats[:, so + 6:so + 7],
            stats[:, 15:16], stats[:, so + 2:so + 3], ALU.mult,
            ALU.subtract), waits=_later(e_msq, e_sq, ev_invd))
        e_ln = act(lambda e: e.activation(
            stats[:, so + 4:so + 5], stats[:, so + 3:so + 4], AF.Ln,
            bias=stats[:, 7:8]), waits=_later(e_var, ev_eps))
        e_rstd = act(lambda e: e.activation(
            stats[:, so + 4:so + 5], stats[:, so + 4:so + 5], AF.Exp,
            scale=-0.5), waits=_later(e_ln))
        e_nmrs = dve(lambda e: e.tensor_scalar(
            stats[:, so + 5:so + 6], stats[:, so + 1:so + 2],
            stats[:, so + 4:so + 5], -1.0, ALU.mult, ALU.mult),
            waits=_later(e_rstd))
        e_ap = act(lambda e: e.activation(
            out_ap, xap, AF.Identity, bias=stats[:, so + 5:so + 6],
            scale=stats[:, so + 4:so + 5]), waits=_later(e_nmrs))
        return e_ap, e_rs

    ev_eqx = {}
    pend_qktr = [None]
    pend_rope = [None]
    rope_o2 = {}

    def flush_qktr():
        if pend_qktr[0] is not None:
            pend_qktr[0]()
            pend_qktr[0] = None

    def flush_rope():
        if pend_rope[0] is not None:
            pend_rope[0]()
            pend_rope[0] = None

    for tt in range(min(4, TT)):
        ldx[tt] = dma("sync", xA[tt % 4][:, :],
                      x_d[tt * 128:(tt + 1) * 128, :], sem=f"sxa{tt % 4}")
    ev_lntr = {}

    def emit_chain(tt):
        xb = tt % 4
        so = (tt % 4) * 8
        wx = _later(ldx[tt], ev_lnap.get(tt - 4), ev_red.get(tt - 4))
        e_lnb, e_rs = ln_chain(xA[xb][:, :], so, wx, ln_bf[:, :],
                               atmpA[:, :])
        ev_lnap[tt] = e_lnb
        ev_red[tt] = e_rs
        if tt + 4 < TT:
            ldx[tt + 4] = dma("sync", xA[xb][:, :],
                              x_d[(tt + 4) * 128:(tt + 5) * 128, :],
                              waits=_later(e_lnb, e_rs), sem=f"sxa{xb}")
    emit_chain(0)
    for tt in range(TT):
        xb = tt % 4
        qkb = [qk_sb, qk2_sb][tt % 2]
        e_lnb = ev_lnap[tt]
        e_trg = [None, None]
        for g in range(2):
            wts = _later(e_lnb, ev_lntr.get(tt - 1),
                         ev_ini if tt == 0 else None)
            for j in range(8):
                e_trg[g] = tr_ln(
                    TRP[:, g * 1024 + j * 128:g * 1024 + (j + 1) * 128],
                    ln_bf[:, (8 * g + j) * 128:(8 * g + j + 1) * 128],
                    waits=wts, inc="pe" if j == 7 else None)
                wts = ()
        e_ev1 = dve(lambda e: e.tensor_copy(lnT_sb[:, 0:1024],
                                            TRP[:, 0:1024]),
                    waits=_later(e_trg[0]))
        e_ev2 = dve(lambda e: e.tensor_copy(lnT_sb[:, 1024:2048],
                                            TRP[:, 1024:2048]),
                    waits=_later(e_trg[1]))
        ev_lntr[tt] = e_ev2
        if tt + 1 < TT:
            emit_chain(tt + 1)
        flush_rope()
        segw = [512, 512, 512, 512, 256]
        ev_seg = [None] * 5
        for seg in range(5):
            pb = SEG[:, (seg % 2) * 512:(seg % 2) * 512 + segw[seg]]
            guard = {0: ev_half.get((tt - 1, 0)), 1: ev_half.get((tt - 1, 1)),
                     2: ev_half.get((tt, 0)), 3: ev_half.get((tt, 1)),
                     4: ev_half.get((tt, 0))}[seg]
            for ec in range(16):
                wts = _later(e_ev1, e_ev2, ev_wA, guard) if ec == 0 else ()
                ev_seg[seg] = mm(
                    pb, lnT_sb[:, ec * 128:(ec + 1) * 128],
                    wA_sb[:, ec * 2304 + seg * 512:
                          ec * 2304 + seg * 512 + segw[seg]],
                    ec == 0, ec == 15, waits=wts,
                    inc="pe" if ec == 15 else None)
            if seg == 1:
                flush_qktr()
            if seg < 4:
                e_cp = dve(lambda e, seg=seg, qkb=qkb: e.tensor_copy(
                    qkb[:, seg * 512:(seg + 1) * 512],
                    SEG[:, (seg % 2) * 512:(seg % 2) * 512 + 512]),
                    waits=_later(ev_seg[seg], ev_trqk.get(tt - 2)))
                ev_half[(tt, seg % 2)] = e_cp
            else:
                e_cpk = dve(lambda e, qkb=qkb: e.tensor_copy(
                    qkb[:, 2048:2176], SEG[:, 0:128]),
                    waits=_later(ev_seg[4], ev_trqk.get(tt - 2)))
                e_vc = dve(lambda e, tt=tt: e.tensor_copy(
                    V_sb[:, tt * 128:(tt + 1) * 128], SEG[:, 128:256]),
                    waits=_later(ev_seg[4]))
                ev_half[(tt, 0)] = e_vc
                ev_vc[tt] = e_vc
        def rope_emit(tt=tt, qkb=qkb, ev_h0=ev_half[(tt, 0)],
                      ev_h1=ev_half[(tt, 1)]):
            qk3 = qkb[:, :].rearrange("p (h d) -> p h d", d=128)
            A1 = qk3[:, :, 0:64]
            A2 = qk3[:, :, 64:128]
            cosb = cos_sb[:, tt * 64:(tt + 1) * 64].unsqueeze(1) \
                .broadcast_to((128, 17, 64))
            sinb = sin_sb[:, tt * 64:(tt + 1) * 64].unsqueeze(1) \
                .broadcast_to((128, 17, 64))
            rtv = [r[:, :].rearrange("p (h d) -> p h d", d=64) for r in rt]
            e_r1 = dve(lambda e, a=A1, b=cosb: e.tensor_tensor(
                rtv[0], a, b, ALU.mult),
                waits=_later(ev_h0, ev_h1, ev_ini))
            e_r2 = dve(lambda e, a=A2, b=sinb: e.tensor_tensor(
                rtv[1], a, b, ALU.mult))
            e_r3 = dve(lambda e, a=A1, b=sinb: e.tensor_tensor(
                rtv[2], a, b, ALU.mult))
            e_r4 = dve(lambda e, a=A2, b=cosb: e.tensor_tensor(
                rtv[3], a, b, ALU.mult))
            e_o1 = dve(lambda e, a=A1: e.tensor_tensor(
                a, rtv[0], rtv[1], ALU.subtract),
                waits=_later(e_r1, e_r2, e_r3, e_r4))
            e_o2 = dve(lambda e, a=A2: e.tensor_tensor(
                a, rtv[2], rtv[3], ALU.add), waits=_later(e_o1))

            rope_o2[tt] = e_o2

        def qktr(tt=tt, qkb=qkb):
            e_o2 = rope_o2[tt]
            gloc = tt % 4
            half = (tt // 4) % 2
            qgv = qstg[:, half * 8192:(half + 1) * 8192] \
                .rearrange("p (h c) -> p h c", c=512)
            prev = ev_eqx.get(tt - 1)
            gev = [None] * 3
            wts = _later(e_o2, prev[1] if prev else None,
                         prev[2] if prev else None)
            for j in range(8):
                gev[0] = tr_qk(TRP2[:, j * 128:(j + 1) * 128],
                            qkb[:, j * 128:(j + 1) * 128],
                            waits=wts, inc="pe" if j == 7 else None)
                wts = ()
            e_eq0 = dve(lambda e, gloc=gloc, qgv=qgv: e.tensor_copy(
                qgv[:, 0:8, gloc * 128:(gloc + 1) * 128],
                TRP2[:, 0:1024].rearrange("p (h c) -> p h c", c=128)),
                waits=_later(gev[0], ev_qfl.get(tt // 4 - 2)))
            wts = ()
            for j in range(8):
                gev[1] = tr_qk(TRP2[:, 1024 + j * 128:1024 + (j + 1) * 128],
                            qkb[:, (8 + j) * 128:(9 + j) * 128],
                            waits=wts, inc="pe" if j == 7 else None)
            e_eq1 = dve(lambda e, gloc=gloc, qgv=qgv: e.tensor_copy(
                qgv[:, 8:16, gloc * 128:(gloc + 1) * 128],
                TRP2[:, 1024:2048].rearrange("p (h c) -> p h c", c=128)),
                waits=_later(gev[1]))
            gev[2] = tr_qk(TRP2[:, 0:128], qkb[:, 16 * 128:17 * 128],
                        waits=_later(e_eq0), inc="pe")
            e_ek = dve(lambda e, tt=tt: e.tensor_copy(
                K_sb[:, tt * 128:(tt + 1) * 128], TRP2[:, 0:128]),
                waits=_later(gev[2]))
            ev_kc[tt] = e_ek
            ev_trqk[tt] = gev[2]
            ev_eqx[tt] = (e_eq0, e_eq1, e_ek)
            if tt % 4 == 3:
                g4 = tt // 4
                evs = _later(e_eq0, e_eq1, e_ek)
                for h in range(NH):
                    ev_qfl[g4] = dma(
                        "scalar", q_scr[h, :, g4 * 512:(g4 + 1) * 512],
                        qstg[:, half * 8192 + h * 512:
                             half * 8192 + (h + 1) * 512],
                        waits=evs if h == 0 else (), sem="sqf",
                        serial=False)
        pend_rope[0] = rope_emit
        pend_qktr[0] = qktr
        emit_casts(2, waits=_later(ldx[tt]))
    flush_rope()
    flush_qktr()
    emit_casts(1000)
    ev_wo = cast_ev["sgW"]
    ev_p1c = cast_ev["sgP1"]
    ev_p2c = cast_ev["sgP2"]

    PEA = ("pe", S.counts.get("pe", 0))
    DVEA = ("dve", S.counts.get("dve", 0))
    ACTA = ("act", S.counts.get("act", 0))
    SQFA = ("sqf", S.counts.get("sqf", 0))
    G0 = (PEA, DVEA, ACTA, SQFA)

    ev_exp = {}
    ev_sc2 = {}
    ring = {s: [G0] for s in range(8)}
    ev_qsld = {}
    qs_guard = {0: G0, 1: G0}
    ev_avevac = {}
    ev_dmm = {}
    ev_rc = {}
    ev_aT = {}
    ev_mm1 = {}
    ev_gelu = {}
    ev_cp1 = {}
    ev_p1ld = {}
    p1_guard = {0: G0, 1: G0}
    wr_guard = {0: G0, 1: G0}
    wr_i = [0]
    ev_mm2ep = {}
    ev_ysto = {}
    ev_opj = {}
    ev_mm2last = {}
    ev_x1w = {}
    ev_stgdone = {}
    ev_xld = {}
    x_guard = {0: G0, 1: G0}
    ln2slot_guard = {0: G0, 1: G0}
    trc_guard = {0: G0, 1: G0}
    att_tail = {}
    b2c_ev = [None]

    pend_gelu = [None]
    pend_tail = [None, None]

    def flush_gelu():
        if pend_gelu[0] is not None:
            pend_gelu[0]()
            pend_gelu[0] = None

    def flush_tail(i=None):
        for j in ((0, 1) if i is None else (i,)):
            if pend_tail[j] is not None:
                pend_tail[j]()
                pend_tail[j] = None

    def emit_av(c, h, p, s_adds):
        h_g = c * NH + h
        g = h_g * NP + p
        slot = g % 8
        evav = None
        for kk in (2 * p, 2 * p + 1):
            w = ()
            if kk == 2 * p:
                w = _later(ev_exp[g],
                           ev_avevac.get(h_g - 1) if p == 0 else None,
                           G0 if h_g == 0 and p == 0 else None)
            evav = mm_avv(AVp, V_sb[:, kk * 128:(kk + 1) * 128],
                      E_sb[:, slot * 1024 + (kk - 2 * p) * 512:
                           slot * 1024 + (kk - 2 * p + 1) * 512],
                      p == 0 and kk == 2 * p,
                      p == NP - 1 and kk == 2 * p + 1,
                      waits=w, inc="pe")
        ring[slot] = [evav, s_adds[p]]
        return evav

    def emit_head(c, h, take_fill):
        h_g = c * NH + h
        qpar = h % 2
        Sb = SS[h % 2]
        if (c, h) not in ev_qsld:
            ev_qsld[(c, h)] = dma(
                "sync", Qs[qpar][:, :], q_scr[h, :, c * 512:(c + 1) * 512],
                waits=_later(qs_guard[qpar],
                             G0 if c == 0 and h <= 1 else None),
                sem=f"sq{qpar}")
        nx_c, nx_h = (c, h + 1) if h + 1 < NH else (c + 1, 0)
        if nx_c < NCH and (nx_c, nx_h) not in ev_qsld:
            ev_qsld[(nx_c, nx_h)] = dma(
                "sync", Qs[nx_h % 2][:, :],
                q_scr[nx_h, :, nx_c * 512:(nx_c + 1) * 512],
                waits=_later(qs_guard[nx_h % 2],
                             G0 if c == 0 and h == 0 else None),
                sem=f"sq{nx_h % 2}")
        flush_tail(0)
        s_adds = {}
        evav = [None]
        for p in range(NP):
            g = h_g * NP + p
            slot = g % 8
            pp = PP[g % 2]
            w0 = _later(ev_qsld[(c, h)], ev_exp.get(g - 2),
                        ev_stgdone.get(c - 1) if (h == 0 and p == 0)
                        else None,
                        G0 if (c == 0 and h == 0 and p == 0) else None)
            mm_sc(pp[:, 0:512], K_sb[:, (2 * p) * 128:(2 * p + 1) * 128],
                  Qs[qpar][:, :], True, True, waits=w0)
            ev_sc2[g] = mm_sc(pp[:, 512:1024],
                           K_sb[:, (2 * p + 1) * 128:(2 * p + 2) * 128],
                           Qs[qpar][:, :], True, True, inc="pe")
            if p == 2:
                flush_tail(1)
            if p >= 2:
                evav[0] = emit_av(c, h, p - 2, s_adds)
                take_fill(4)
            ev_exp[g] = act(
                lambda e, pp=pp, slot=slot: e.activation(
                    E_sb[:, slot * 1024:(slot + 1) * 1024], pp[:, :],
                    AF.Exp, scale=ISQ),
                waits=_later(ev_sc2[g], *ring[slot]))
            if p == 0:
                s_adds[p] = dve(lambda e, slot=slot, Sb=Sb: e.tensor_tensor(
                    Sb[:, :], E_sb[:, slot * 1024:slot * 1024 + 512],
                    E_sb[:, slot * 1024 + 512:(slot + 1) * 1024], ALU.add),
                    waits=_later(ev_exp[g], ev_dmm.get(h_g - 2)))
            else:
                e_pl = pool(lambda e, slot=slot: e.tensor_tensor(
                    tmpP[:, :], E_sb[:, slot * 1024:slot * 1024 + 512],
                    E_sb[:, slot * 1024 + 512:(slot + 1) * 1024], ALU.add),
                    waits=_later(ev_exp[g], s_adds.get(p - 1)))
                s_adds[p] = dve(lambda e, Sb=Sb: e.tensor_tensor(
                    Sb[:, :], Sb[:, :], tmpP[:, :], ALU.add),
                    waits=_later(e_pl))
        evav[0] = emit_av(c, h, NP - 2, s_adds)
        take_fill(4)
        evav[0] = emit_av(c, h, NP - 1, s_adds)
        take_fill(4)
        qs_guard[qpar] = ev_sc2[h_g * NP + NP - 1]

        def tail0(h_g=h_g, evav=evav[0]):
            ev_avevac[h_g] = dve(lambda e: e.tensor_copy(
                avs[:, :], AVp[:, :]), waits=_later(evav))

        def tail1(h_g=h_g, h=h, c=c, Sb=Sb, s_last=s_adds[NP - 1]):
            ev_dmm[h_g] = mm(B7[:, :], ones128_sb[:, :], Sb[:, :],
                             True, True,
                             waits=_later(s_last, ev_rc.get(h_g - 1),
                                          b2c_ev[0] if h == 0 else None,
                                          G0 if h_g == 0 else None),
                             inc="pe")
            ev_rc[h_g] = dve(lambda e: e.reciprocal(rD_sb[:, :], B7[:, :]),
                             waits=_later(ev_dmm[h_g]))
            ev_aT[h_g] = dve(lambda e, h=h: e.tensor_tensor(
                aT_sb[:, h * 512:(h + 1) * 512], avs[:, :], rD_sb[:, :],
                ALU.mult),
                waits=_later(ev_opj.get(c - 1) if h == 0 else None,
                             G0 if h_g == 0 else None))
        pend_tail[0] = tail0
        pend_tail[1] = tail1

    def filler_stream(cf):
        for ht in range(NHT):
            pb = ht % 2
            for ec in range(16):
                def emit_one(ht=ht, ec=ec, pb=pb):
                    if ec == 0:
                        flush_gelu()
                        for hh in (ht, ht + 1):
                            if hh < NHT and (cf, hh) not in ev_p1ld:
                                ev_p1ld[(cf, hh)] = dma(
                                    "sync",
                                    p1s[hh % 2][:, :], p1_bf[hh],
                                    waits=_later(p1_guard[hh % 2],
                                                 ev_p1c if cf == 0 and
                                                 hh <= 1 else None),
                                    sem=f"sp1{hh % 2}")
                        w0 = _later(ev_p1ld[(cf, ht)],
                                    ev_cp1.get((cf, ht - 2)),
                                    ev_stgdone.get(cf),
                                    G0 if cf == 0 and ht <= 1 else None)
                    else:
                        w0 = ()
                    ev = mm_f1(M1[pb], p1s[pb][:, ec * 128:(ec + 1) * 128],
                            stg[:, ec * 512:(ec + 1) * 512],
                            ec == 0, ec == 15, waits=w0,
                            inc="pe" if ec == 15 else None)
                    if ec == 15:
                        ev_mm1[(cf, ht)] = ev
                        p1_guard[pb] = ev
                        ev_cp1[(cf, ht)] = dve(
                            lambda e, ht=ht, pb=pb: e.tensor_copy(
                                hT_sb[:, ht * 512:(ht + 1) * 512], M1[pb]),
                            waits=_later(ev, G0 if cf == 0 else None))
                yield emit_one

    def emit_mm2(cf):
        PSs = [PA, PB]
        tail = att_tail.get(cf + 1, att_tail[cf])
        stop_ev = {}
        for ht in range(NHT):
            ev_gelu[(cf, ht)] = act(
                lambda e, ht=ht: e.activation(
                    hT_sb[:, ht * 512:(ht + 1) * 512],
                    hT_sb[:, ht * 512:(ht + 1) * 512], AF.Gelu,
                    bias=b1r_sb[:, ht:ht + 1]),
                waits=_later(ev_cp1[(cf, ht)], ev_ini))
        W6 = [wr[0], wr[1]] + [stg[:, i * 2048:(i + 1) * 2048]
                               for i in range(4)]
        NW = 6
        ldev = {}
        grp_last = {}

        def issue_load(j):
            hcj = j % NHT
            sl = j % NW
            w = _later(grp_last.get(j - NW),
                       wr_guard[sl] if (j < NW and sl < 2) else None,
                       ev_mm1[(cf, NHT - 1)] if (j < NW and sl >= 2)
                       else None,
                       ev_p2c if cf == 0 and j < NW else None)
            ldev[j] = dma(("sync", "gpsimd")[sl % 2], W6[sl][:, :],
                          p2_bf[hcj * 128:(hcj + 1) * 128, :], waits=w,
                          sem=f"swr{sl % 2}")
        for j in range(NW):
            issue_load(j)
        for pr in range(2):
            tls = (2, 3) if pr == 0 else (0, 1)
            if pr == 1:
                for tl in (0, 1):
                    ev_x1r = dma(
                        "gpsimd", x1t[tl % 2][:, :],
                        x1_scr[(cf * 4 + tl) * 128:(cf * 4 + tl + 1) * 128,
                               :],
                        waits=_later(ev_x1w[(cf, tl)],
                                     ev_mm2ep[(cf, tl + 2)]),
                        sem=f"sx1r{tl % 2}")
                    ev_x1w[(cf, tl, "r")] = ev_x1r
            for hc in range(NHT):
                j = pr * NHT + hc
                sl = j % NW
                ev_w = ldev[j]
                last = None
                for ti, tl in enumerate(tls):
                    ps = PSs[ti]
                    for s4 in range(4):
                        w = ()
                        if hc == 0 and s4 == 0:
                            if pr == 0:
                                w = _later(ev_w, ev_gelu[(cf, hc)],
                                           tail[0], tail[1],
                                           ev_stgdone[cf],
                                           ev_gelu[(cf, NHT - 1)]
                                           if ti == 1 else None)
                            else:
                                w = _later(ev_w, ev_mm2ep[(cf, tls[ti] + 2)])
                        elif s4 == 0:
                            w = _later(ev_w, ev_gelu[(cf, hc)])
                        last = mm_f2(ps[:, s4 * 512:(s4 + 1) * 512],
                                     hT_sb[:, hc * 512 + tl * 128:
                                     hc * 512 + (tl + 1) * 128],
                                     W6[sl][:, s4 * 512:(s4 + 1) * 512],
                                     hc == 0, hc == NHT - 1, waits=w,
                                     inc="pe" if (s4 == 3 and
                                                  (ti == 1 or
                                                   hc == NHT - 1))
                                     else None)
                        if s4 == 3 and hc == NHT - 1:
                            stop_ev[tl] = last
                grp_last[j] = last
                if j + NW < 2 * NHT:
                    issue_load(j + NW)
            for ti, tl in enumerate(tls):
                yv = y_stage[:, ti * 2048:(ti + 1) * 2048]
                wts = _later(stop_ev[tl],
                             ev_x1w.get((cf, tl, "r")),
                             ev_ysto.get((cf, tl + 2)),
                             ev_ysto.get((cf - 1, tl - 2)) if pr == 0
                             else None,
                             tail[0] if pr == 0 else None)
                ev_mm2ep[(cf, tl)] = dve(
                    lambda e, yv=yv, ps=PSs[ti], tl=tl: e.tensor_tensor(
                        yv, ps[:, :], x1t[tl % 2][:, :], ALU.add),
                    waits=wts)
                ev_ysto[(cf, tl)] = dma(
                    "scalar",
                    y_d[(cf * 4 + tl) * 128:(cf * 4 + tl + 1) * 128, :],
                    yv, waits=_later(ev_mm2ep[(cf, tl)]), sem="ssy",
                    serial=False)
        wr_guard[0] = grp_last[126]
        wr_guard[1] = grp_last[127]
        ev_mm2last[cf] = stop_ev[1]
        for s in range(8):
            ring[s] = ring[s] + [ev_ysto[(cf, 2 if s < 4 else 3)],
                                 ev_ysto[(cf, 0 if s < 4 else 1)]]

    def emit_outproj(c):
        PSs = [PA, PB]
        if b2c_ev[0] is None:
            e_l = dma("sync", b2b_sb[0:1, :], b2row_d[:, :],
                      waits=_later(G0), sem="sini", serial=False)
            e_c = None
            for s4 in range(4):
                e_m = mm(B7[:, :], onesr_sb[0:1, :],
                         b2b_sb[0:1, s4 * 512:(s4 + 1) * 512], True, True,
                         waits=_later(e_l, ev_aT[c * NH + NH - 1],
                                      e_c), inc="pe")
                e_c = dve(lambda e, s4=s4: e.tensor_copy(
                    b2b_sb[:, s4 * 512:(s4 + 1) * 512], B7[:, :]),
                    waits=_later(e_m))
            b2c_ev[0] = e_c
        for tl in (0, 1):
            ev_xld[(c, tl)] = dma(
                "gpsimd", x_sb[tl % 2][:, :],
                x_d[(c * 4 + tl) * 128:(c * 4 + tl + 1) * 128, :],
                waits=_later(x_guard[tl % 2], G0 if c == 0 else None),
                sem=f"sx{tl % 2}")
        stop_ev = {}
        ev_x1add = {}
        ev_l2ap = {}
        ev_b2ad = {}
        e_evc = {}
        stgv = stg[:, :].rearrange("p (e c) -> p e c", c=512)

        W4 = [wr[0], wr[1], p1s[0], p1s[1]]
        W4sem = ["swr0", "swr1", "sp10", "sp11"]
        ldev = {}
        grp_last = {}

        def issue_load(j):
            dhj = j % 16
            sl = j % 4
            w = _later(grp_last.get(j - 4),
                       (wr_guard[sl] if sl < 2 else
                        ev_mm1.get((c - 1, NHT - 1))) if j < 4 else None,
                       G0 if c == 0 and j < 4 else None,
                       ev_wo if c == 0 and j < 4 else None)
            ldev[j] = dma(("sync", "gpsimd")[sl % 2], W4[sl][:, :],
                          woT_bf[dhj * 128:(dhj + 1) * 128, :], waits=w,
                          sem=W4sem[sl])
        for j in range(4):
            issue_load(j)

        def emit_pr(pr):
            tls = (pr * 2, pr * 2 + 1)
            for dh in range(16):
                j = pr * 16 + dh
                sl = j % 4
                ev_w = ldev[j]
                last = None
                for ti, tl in enumerate(tls):
                    ps = PSs[ti]
                    for s4 in range(4):
                        w = ()
                        if dh == 0 and s4 == 0:
                            if pr == 0:
                                w = _later(ev_w, ev_aT[c * NH + NH - 1],
                                           att_tail[c][0],
                                           ev_mm2ep.get((c - 1, 1)),
                                           b2c_ev[0])
                            else:
                                w = _later(ev_w, ev_x1add[tls[ti] - 2])
                        elif s4 == 0:
                            w = _later(ev_w)
                        last = mm_op(ps[:, s4 * 512:(s4 + 1) * 512],
                                     aT_sb[:, dh * 512 + tl * 128:
                                     dh * 512 + (tl + 1) * 128],
                                     W4[sl][:, s4 * 512:(s4 + 1) * 512],
                                     dh == 0, dh == 15, waits=w,
                                     inc="pe" if (s4 == 3 and
                                                  (ti == 1 or dh == 15))
                                     else None)
                        if s4 == 3 and dh == 15:
                            stop_ev[tl] = last
                grp_last[j] = last
                if j + 4 < 32:
                    issue_load(j + 4)
            ev_opj[c] = stop_ev[tls[1]]

        def x1_part(tl):
            xb = tl % 2
            ev_x1add[tl] = dve(
                lambda e, xb=xb, ps=PSs[tl % 2]: e.tensor_tensor(
                    x1t[xb][:, :], ps[:, :], x_sb[xb][:, :], ALU.add),
                waits=_later(stop_ev[tl], ev_xld[(c, tl)],
                             ev_x1w.get((c, tl - 2)),
                             ev_b2ad.get(tl - 2),
                             G0 if c == 0 else None))
            x_guard[xb] = ev_x1add[tl]
            if tl + 2 <= 3:
                ev_xld[(c, tl + 2)] = dma(
                    "gpsimd", x_sb[xb][:, :],
                    x_d[(c * 4 + tl + 2) * 128:(c * 4 + tl + 3) * 128, :],
                    waits=_later(ev_x1add[tl]), sem=f"sx{xb}")

        def ln2_part(tl):
            xb = tl % 2
            so = xb * 8
            wx = _later(ev_x1add[tl], ln2slot_guard[xb],
                        G0 if c == 0 else None)
            e_l2, _ = ln_chain(
                x1t[xb][:, :], so, wx,
                ln2b4[:, xb * 2048:(xb + 1) * 2048],
                ln2b4[:, xb * 2048:(xb + 1) * 2048])
            ev_l2ap[tl] = e_l2
            ev_b2ad[tl] = dve(
                lambda e, xb=xb: e.tensor_tensor(
                    x1t[xb][:, :], x1t[xb][:, :], b2b_sb[:, :], ALU.add),
                waits=_later(e_l2, b2c_ev[0]))
            if tl <= 1:
                ev_x1w[(c, tl)] = dma(
                    "gpsimd",
                    x1_scr[(c * 4 + tl) * 128:(c * 4 + tl + 1) * 128, :],
                    x1t[xb][:, :], waits=_later(ev_b2ad[tl]),
                    sem="sx1w", serial=False)

        def tr_part(tl):
            xb = tl % 2
            buf = TRC[xb]
            wts = _later(ev_l2ap[tl], ev_x1add[2 + xb], trc_guard[xb])
            e_t = None
            for j in range(16):
                e_t = tr(buf[:, j * 128:(j + 1) * 128],
                         ln2b4[:, xb * 2048 + j * 128:
                               xb * 2048 + (j + 1) * 128],
                         waits=wts, inc="pe" if j == 15 else None)
                wts = ()
            ln2slot_guard[xb] = e_t
            e_evc[tl] = dve(lambda e, tl=tl, buf=buf: e.tensor_copy(
                stgv[:, :, tl * 128:(tl + 1) * 128],
                buf[:, :].rearrange("p (e c) -> p e c", c=128)),
                waits=_later(e_t, ev_mm1.get((c - 1, NHT - 1)),
                             ev_mm2last.get(c - 1)))
            trc_guard[xb] = e_evc[tl]

        emit_pr(0)
        x1_part(0)
        x1_part(1)
        ln2_part(0)
        ln2_part(1)
        emit_pr(1)
        x1_part(2)
        x1_part(3)
        tr_part(0)
        tr_part(1)
        ln2_part(2)
        ln2_part(3)
        tr_part(2)
        tr_part(3)
        ev_stgdone[c] = e_evc[3]
        wr_guard[0] = grp_last[28]
        wr_guard[1] = grp_last[29]
        p1_guard[0] = grp_last[30]
        p1_guard[1] = grp_last[31]

    for k in range(NCH + 1):
        cA, cF = k, k - 1
        fs = filler_stream(cF) if cF >= 0 else iter(())

        def take_fill(n, fs=fs):
            for _ in range(n):
                try:
                    next(fs)()
                except StopIteration:
                    break
        if cA < NCH:
            for h in range(NH):
                emit_head(cA, h, take_fill)
            flush_tail()
            att_tail[cA] = (ev_exp[(cA * NH + NH - 1) * NP + NP - 1],
                            ev_aT[cA * NH + NH - 1])
        for f in fs:
            f()
        flush_gelu()
        if cF >= 0:
            emit_mm2(cF)
        if cA < NCH:
            emit_outproj(cA)

    dma_sems = [k for k in S.counts if k not in ("pe", "act", "dve")]
    finals = tuple((k, S.counts[k]) for k in dma_sems)
    S.emit("scalar", lambda e: e.nop(),
           waits=tuple((k, v) for k, v in finals
                       if k in ("ssy", "sx1w", "sqf")))
    S.emit("sync", lambda e: e.nop(), waits=finals)

    with nc.Block() as block:
        S.replay(block)
    nc.finalize()
    return nc


def _prep(w_attn, w_out, proj1, proj1_bias, proj2, proj2_bias, T):
    TT = T // 128
    d = {}
    d["wqkvT"] = np.ascontiguousarray(w_attn.T)
    d["woT"] = np.ascontiguousarray(w_out.T)
    d["p1blk"] = np.ascontiguousarray(
        proj1.reshape(NHT, 128, 16, 128).transpose(0, 3, 2, 1)
        .reshape(NHT, 128, 2048))
    d["p2T"] = np.ascontiguousarray(proj2.T)
    d["b1r"] = np.ascontiguousarray(proj1_bias.reshape(NHT, 128).T)
    d["b2row"] = np.ascontiguousarray(proj2_bias.reshape(1, D))
    j = np.arange(0, HD, 2, dtype=np.float64)
    pos = (10000.0 ** ((-2.0 * j - 1.0) / HD)).astype(np.float32)
    ang = np.arange(T, dtype=np.float32)[:, None] * pos[None, :]
    cos = np.cos(ang).astype(np.float32)
    sin = np.sin(ang).astype(np.float32)
    d["cosA"] = np.ascontiguousarray(
        cos.reshape(TT, 128, 64).transpose(1, 0, 2).reshape(128, TT * 64))
    d["sinA"] = np.ascontiguousarray(
        sin.reshape(TT, 128, 64).transpose(1, 0, 2).reshape(128, TT * 64))
    d["ident"] = np.eye(128, dtype=ml_dtypes.bfloat16)
    d["onescol"] = np.ones((128, 1), dtype=np.float32)
    d["ones128"] = np.ones((128, 128), dtype=np.float32)
    d["onesr"] = np.ones((1, 128), dtype=np.float32)
    return d


def kernel(L0_attn_x, L0_attn_w_attn, L0_attn_w_out, L0_ffn_proj1,
           L0_ffn_proj1_bias, L0_ffn_proj2, L0_ffn_proj2_bias,
           L0_ffn_gate_w=None, **_unused):
    x = np.asarray(L0_attn_x, dtype=np.float32)
    B, T, _ = x.shape
    shared = _prep(np.asarray(L0_attn_w_attn, np.float32),
                   np.asarray(L0_attn_w_out, np.float32),
                   np.asarray(L0_ffn_proj1, np.float32),
                   np.asarray(L0_ffn_proj1_bias, np.float32),
                   np.asarray(L0_ffn_proj2, np.float32),
                   np.asarray(L0_ffn_proj2_bias, np.float32), T)
    nc = build_program(T)
    in_maps = []
    for b in range(B):
        m = dict(shared)
        m["x"] = np.ascontiguousarray(x[b])
        in_maps.append(m)
    res = run_bass_kernel_spmd(nc, in_maps, core_ids=list(range(B)))
    out = np.stack([np.asarray(r["y"], dtype=np.float32)
                    for r in res.results], axis=0)
    return out


# version 18
# speedup vs baseline: 1.0259x; 1.0259x over previous
import numpy as np
import ml_dtypes
import concourse.bass as bass
import concourse.mybir as mybir
from concourse.bass_utils import run_bass_kernel_spmd

F32 = mybir.dt.float32
BF16 = mybir.dt.bfloat16
AF = mybir.ActivationFunctionType
ALU = mybir.AluOpType
AX = mybir.AxisListType

D = 2048
NH = 16
HD = 128
HID = 8192
NHT = HID // 128
EPS = 1e-5
ISQ = float(1.0 / np.sqrt(128.0))

ENGINES = ("tensor", "scalar", "vector", "sync", "gpsimd")


class Sched:

    def __init__(self, nc):
        self.nc = nc
        self.items = {e: [] for e in ENGINES}
        self.sems = {}
        self.counts = {}

    def _sem(self, name):
        if name not in self.sems:
            self.sems[name] = self.nc.alloc_semaphore(name=name)
            self.counts[name] = 0
        return self.sems[name]

    def emit(self, eng, fn, waits=(), inc=None, inc_by=1):
        ev = None
        if inc is not None:
            self._sem(inc)
            self.counts[inc] += inc_by
            ev = (inc, self.counts[inc])
        ws = tuple((s, v) for (s, v) in waits if s is not None and v > 0)
        for s, _ in ws:
            self._sem(s)
        self.items[eng].append((fn, ws, inc, inc_by))
        return ev

    def replay(self, block):
        for ename in ENGINES:
            seq = self.items[ename]
            if not seq:
                continue

            def mk(seq):
                def body(eng):
                    wm = {}
                    for fn, waits, inc, inc_by in seq:
                        for (sname, val) in waits:
                            if wm.get(sname, 0) >= val:
                                continue
                            eng.wait_ge(self.sems[sname], val)
                            wm[sname] = val
                        ins = fn(eng)
                        if inc is not None:
                            ins.then_inc(self.sems[inc], inc_by)
                return body
            getattr(block, ename)(mk(seq))


def _later(*evs):
    best = {}

    def add(ev):
        if ev is None:
            return
        if isinstance(ev, tuple) and len(ev) == 2 and isinstance(ev[0], str):
            s, v = ev
            if best.get(s, -1) < v:
                best[s] = v
        else:
            for e in ev:
                add(e)
    for ev in evs:
        add(ev)
    return tuple(best.items())


def build_program(T):
    TT = T // 128
    KT = T // 128
    NP = KT // 2
    NCH = T // 512
    assert TT % 4 == 0

    nc = bass.Bass("TRN2", target_bir_lowering=False)

    x_d = nc.dram_tensor("x", [T, D], F32, kind="ExternalInput")
    wqkvT_d = nc.dram_tensor("wqkvT", [D, 2304], F32, kind="ExternalInput")
    woT_d = nc.dram_tensor("woT", [D, D], F32, kind="ExternalInput")
    p1blk_d = nc.dram_tensor("p1blk", [NHT, 128, 2048], F32,
                             kind="ExternalInput")
    p2T_d = nc.dram_tensor("p2T", [HID, D], F32, kind="ExternalInput")
    b1r_d = nc.dram_tensor("b1r", [128, NHT], F32, kind="ExternalInput")
    b2row_d = nc.dram_tensor("b2row", [1, D], F32, kind="ExternalInput")
    cosA_d = nc.dram_tensor("cosA", [128, TT * 64], F32, kind="ExternalInput")
    sinA_d = nc.dram_tensor("sinA", [128, TT * 64], F32, kind="ExternalInput")
    ident_d = nc.dram_tensor("ident", [128, 128], BF16, kind="ExternalInput")
    onescol_d = nc.dram_tensor("onescol", [128, 1], F32, kind="ExternalInput")
    onesr_d = nc.dram_tensor("onesr", [1, 128], F32, kind="ExternalInput")
    ones128_d = nc.dram_tensor("ones128", [128, 128], F32,
                               kind="ExternalInput")
    y_d = nc.dram_tensor("y", [T, D], F32, kind="ExternalOutput")
    woT_bf = nc.dram_tensor("woT_bf", [D, D], BF16, kind="Internal")
    p1_bf = nc.dram_tensor("p1_bf", [NHT, 128, 2048], BF16, kind="Internal")
    p2_bf = nc.dram_tensor("p2_bf", [HID, D], BF16, kind="Internal")
    q_scr = nc.dram_tensor("q_scr", [NH, 128, T], BF16, kind="Internal")
    x1_scr = nc.dram_tensor("x1_scr", [T, D], F32, kind="Internal")

    ORIGIN = 20480
    LIMIT = 229376 - 128
    off = [ORIGIN]

    def sb(name, cols, dt, at=None, parts=128):
        nbytes = cols * (2 if dt == BF16 else 4)
        if at is None:
            at = off[0]
            off[0] = at + ((nbytes + 31) // 32) * 32
        return nc.alloc_sbuf_tensor_at(name, [parts, cols], dt, offset=at)

    K_sb = sb("K_sb", T, BF16)
    V_sb = sb("V_sb", T, BF16)
    x_sb = [sb("x0", D, F32), sb("x1b", D, F32)]
    stats = sb("stats", 32, F32)
    ident_sb = sb("ident_s", 128, BF16)
    onescol_sb = sb("onescol_s", 1, F32)
    b1r_sb = sb("b1r_s", NHT, F32)
    onesr_sb = sb("onesr_s", 128, F32, parts=1)
    ones128_sb = sb("ones128_s", 128, F32)

    AR0 = off[0]
    E_sb = sb("E_sb", 8 * 1024, BF16, at=AR0)
    y_stage = sb("y_stage", 2 * 2048, F32, at=AR0)
    aT_sb = sb("aT_sb", NH * 512, BF16, at=AR0 + 16384)
    Qs = [sb("Qs0", 512, BF16, at=AR0 + 32768),
          sb("Qs1", 512, BF16, at=AR0 + 33792)]
    hT_sb = sb("hT", NHT * 512, BF16, at=AR0 + 34816)
    stg = sb("stg", 16 * 512, BF16, at=AR0 + 100352)
    ln2b4 = sb("ln2b4", 2 * 2048, BF16, at=AR0 + 116736)
    avs = sb("avs", 512, F32, at=AR0 + 116736)
    p1s = [sb("p1a", 2048, BF16, at=AR0 + 124928),
           sb("p1b", 2048, BF16, at=AR0 + 129024)]
    wr = [sb("wr0", 2048, BF16, at=AR0 + 133120),
          sb("wr1", 2048, BF16, at=AR0 + 137216)]
    x1t = [sb("x1t0", 2048, F32, at=AR0 + 141312),
           sb("x1t1", 2048, F32, at=AR0 + 149504)]
    b2b_sb = sb("b2b", 2048, F32, at=AR0 + 157696)
    SS = [sb("S_sb", 512, F32, at=AR0 + 165888),
          sb("S2_sb", 512, F32, at=AR0 + 167936)]
    tmpP = sb("tmpP", 512, F32, at=AR0 + 169984)
    rD_sb = sb("rD_sb", 512, F32, at=AR0 + 172032)
    CH_END = AR0 + 174080
    wA_sb = sb("wA", 16 * 2304, BF16, at=AR0)
    cos_sb = sb("cosA_s", TT * 64, F32, at=AR0 + 73728)
    sin_sb = sb("sinA_s", TT * 64, F32, at=AR0 + 81920)
    rt = [sb(f"rt{i}", 1088, BF16, at=AR0 + 90112 + i * 2176)
          for i in range(4)]
    qk_sb = sb("qk", 2176, BF16, at=AR0 + 98816)
    lnT_sb = sb("lnT", D, BF16, at=AR0 + 103168)
    ln_bf = sb("ln_bf", D, BF16, at=AR0 + 107264)
    qstg = sb("qstg", 2 * 8192, BF16, at=AR0 + 111360)
    atmpA = sb("atmpA", D, BF16, at=AR0 + 144128)
    xA = [x_sb[0], x_sb[1],
          sb("xA2", D, F32, at=AR0 + 148224),
          sb("xA3", D, F32, at=AR0 + 156416)]
    qk2_sb = sb("qk2", 2176, BF16, at=AR0 + 164608)
    A_END = AR0 + 168960
    assert max(CH_END, A_END) <= LIMIT, f"sbuf overflow {CH_END} > {LIMIT}"

    PA = nc.alloc_psum_tensor("PA", [128, 2048], F32)
    PB = nc.alloc_psum_tensor("PB", [128, 2048], F32)
    PP = [PA[:, 0:1024], PA[:, 1024:2048]]
    AVp = PB[:, 0:512]
    M1 = [PB[:, 512:1024], PB[:, 1024:1536]]
    B7 = PB[:, 1536:2048]
    SEG = PA[:, 0:1024]
    TRP = PB[:, 0:1024].bitcast(BF16)
    TRP2 = PB[:, 1024:2048].bitcast(BF16)
    TRC = [PA[:, 0:1024].bitcast(BF16),
           PB[:, 0:1024].bitcast(BF16)]

    S = Sched(nc)

    def mm(ps, lhsT, rhs, start, stop, waits=(), inc=None):
        return S.emit("tensor",
                      lambda e, o=ps, l=lhsT, r=rhs: e.matmul(
                          o, l, r, start=start, stop=stop),
                      waits=waits, inc=inc)

    def mm_sc(ps, lhsT, rhs, start, stop, waits=(), inc=None):
        return S.emit("tensor", lambda e, o=ps, l=lhsT, r=rhs: e.matmul(
            o, l, r, start=start, stop=stop), waits=waits, inc=inc)

    def mm_avv(ps, lhsT, rhs, start, stop, waits=(), inc=None):
        return S.emit("tensor", lambda e, o=ps, l=lhsT, r=rhs: e.matmul(
            o, l, r, start=start, stop=stop), waits=waits, inc=inc)

    def mm_f1(ps, lhsT, rhs, start, stop, waits=(), inc=None):
        return S.emit("tensor", lambda e, o=ps, l=lhsT, r=rhs: e.matmul(
            o, l, r, start=start, stop=stop), waits=waits, inc=inc)

    def mm_f2(ps, lhsT, rhs, start, stop, waits=(), inc=None):
        return S.emit("tensor", lambda e, o=ps, l=lhsT, r=rhs: e.matmul(
            o, l, r, start=start, stop=stop), waits=waits, inc=inc)

    def mm_op(ps, lhsT, rhs, start, stop, waits=(), inc=None):
        return S.emit("tensor", lambda e, o=ps, l=lhsT, r=rhs: e.matmul(
            o, l, r, start=start, stop=stop), waits=waits, inc=inc)

    def tr(ps, in_, waits=(), inc=None):
        return S.emit("tensor",
                      lambda e, o=ps, i=in_: e.transpose(o, i, ident_sb[:, :]),
                      waits=waits, inc=inc)

    def tr_ln(ps, in_, waits=(), inc=None):
        return S.emit("tensor",
                      lambda e, o=ps, i=in_: e.transpose(o, i, ident_sb[:, :]),
                      waits=waits, inc=inc)

    def tr_qk(ps, in_, waits=(), inc=None):
        return S.emit("tensor",
                      lambda e, o=ps, i=in_: e.transpose(o, i, ident_sb[:, :]),
                      waits=waits, inc=inc)

    def dma(q, out_ap, in_ap, waits=(), sem="lds", serial=True):
        if serial:
            pre = S.counts.get(sem, 0)
            if pre > 0:
                waits = tuple(waits) + ((sem, pre),)
        return S.emit(q, lambda e, o=out_ap, i=in_ap: e.dma_start(o, i),
                      waits=waits, inc=sem, inc_by=16)

    def act(fn, waits=()):
        return S.emit("scalar", fn, waits=waits, inc="act")

    def dve(fn, waits=()):
        return S.emit("vector", fn, waits=waits, inc="dve")

    def pool(fn, waits=()):
        return S.emit("gpsimd", fn, waits=waits, inc="pool")

    ev_wA = None
    for ec in range(16):
        ev_wA = dma("gpsimd", wA_sb[:, ec * 2304:(ec + 1) * 2304],
                    wqkvT_d[ec * 128:(ec + 1) * 128, :], sem="sgA",
                    serial=False)
    cast_pieces = []
    for j in range(16):
        cast_pieces.append((woT_bf[j * 128:(j + 1) * 128, :],
                            woT_d[j * 128:(j + 1) * 128, :], "sgW"))
    for j in range(32):
        k8 = NHT // 32
        cast_pieces.append((p1_bf[j * k8:(j + 1) * k8],
                            p1blk_d[j * k8:(j + 1) * k8], "sgP1"))
    for j in range(32):
        k8 = HID // 32
        cast_pieces.append((p2_bf[j * k8:(j + 1) * k8, :],
                            p2T_d[j * k8:(j + 1) * k8, :], "sgP2"))
    cast_ev = {"sgW": None, "sgP1": None, "sgP2": None}
    cast_i = [0]

    def emit_casts(n, waits=()):
        for _ in range(n):
            if cast_i[0] >= len(cast_pieces):
                return
            dst, srcp, sem = cast_pieces[cast_i[0]]
            cast_i[0] += 1
            cast_ev[sem] = dma("gpsimd", dst, srcp, waits=waits, sem=sem,
                               serial=False)

    ev_ini = None
    for dst, src in [(ident_sb[:, :], ident_d[:, :]),
                     (onescol_sb[:, :], onescol_d[:, :]),
                     (onesr_sb[:, :], onesr_d[:, :]),
                     (ones128_sb[:, :], ones128_d[:, :]),
                     (b1r_sb[:, :], b1r_d[:, :]),
                     (cos_sb[:, :], cosA_d[:, :]),
                     (sin_sb[:, :], sinA_d[:, :])]:
        ev_ini = dma("sync", dst, src, sem="sini", serial=False)
    ev_eps = dve(lambda e: e.memset(stats[:, 7:8], EPS))
    ev_invd = dve(lambda e: e.memset(stats[:, 15:16], 1.0 / D))

    ldx = {}
    ev_lnap = {}
    ev_red = {}
    ev_half = {}
    ev_P1ev = {}
    ev_qfl = {}
    ev_vc = {}
    ev_kc = {}
    ev_trqk = {}

    for tt in range(min(2, TT)):
        ldx[tt] = dma("sync", x_sb[tt % 2][:, :],
                      x_d[tt * 128:(tt + 1) * 128, :], sem=f"sx{tt % 2}")

    def ln_chain(xap, so, w_extra, out_ap, dump_ap):
        e_sq = act(lambda e: e.activation(dump_ap, xap, AF.Square,
                                          accum_out=stats[:, so + 6:so + 7]),
                   waits=w_extra)
        e_rs = dve(lambda e: e.reduce_sum(stats[:, so + 0:so + 1], xap,
                                          axis=AX.X), waits=w_extra)
        e_mean = dve(lambda e: e.tensor_scalar(
            stats[:, so + 1:so + 2], stats[:, so + 0:so + 1], 1.0 / D, None,
            ALU.mult), waits=_later(e_rs))
        e_msq = dve(lambda e: e.tensor_scalar(
            stats[:, so + 2:so + 3], stats[:, so + 1:so + 2],
            stats[:, so + 1:so + 2], None, ALU.mult), waits=_later(e_mean))
        e_var = dve(lambda e: e.scalar_tensor_tensor(
            stats[:, so + 3:so + 4], stats[:, so + 6:so + 7],
            stats[:, 15:16], stats[:, so + 2:so + 3], ALU.mult,
            ALU.subtract), waits=_later(e_msq, e_sq, ev_invd))
        e_ln = act(lambda e: e.activation(
            stats[:, so + 4:so + 5], stats[:, so + 3:so + 4], AF.Ln,
            bias=stats[:, 7:8]), waits=_later(e_var, ev_eps))
        e_rstd = act(lambda e: e.activation(
            stats[:, so + 4:so + 5], stats[:, so + 4:so + 5], AF.Exp,
            scale=-0.5), waits=_later(e_ln))
        e_nmrs = dve(lambda e: e.tensor_scalar(
            stats[:, so + 5:so + 6], stats[:, so + 1:so + 2],
            stats[:, so + 4:so + 5], -1.0, ALU.mult, ALU.mult),
            waits=_later(e_rstd))
        e_ap = act(lambda e: e.activation(
            out_ap, xap, AF.Identity, bias=stats[:, so + 5:so + 6],
            scale=stats[:, so + 4:so + 5]), waits=_later(e_nmrs))
        return e_ap, e_rs

    ev_eqx = {}
    pend_qktr = [None]
    pend_rope = [None]
    rope_o2 = {}

    def flush_qktr():
        if pend_qktr[0] is not None:
            pend_qktr[0]()
            pend_qktr[0] = None

    def flush_rope():
        if pend_rope[0] is not None:
            pend_rope[0]()
            pend_rope[0] = None

    for tt in range(min(4, TT)):
        ldx[tt] = dma("sync", xA[tt % 4][:, :],
                      x_d[tt * 128:(tt + 1) * 128, :], sem=f"sxa{tt % 4}")
    ev_lntr = {}

    def emit_chain(tt):
        xb = tt % 4
        so = (tt % 4) * 8
        wx = _later(ldx[tt], ev_lnap.get(tt - 4), ev_red.get(tt - 4))
        e_lnb, e_rs = ln_chain(xA[xb][:, :], so, wx, ln_bf[:, :],
                               atmpA[:, :])
        ev_lnap[tt] = e_lnb
        ev_red[tt] = e_rs
        if tt + 4 < TT:
            ldx[tt + 4] = dma("sync", xA[xb][:, :],
                              x_d[(tt + 4) * 128:(tt + 5) * 128, :],
                              waits=_later(e_lnb, e_rs), sem=f"sxa{xb}")
    emit_chain(0)
    for tt in range(TT):
        xb = tt % 4
        qkb = [qk_sb, qk2_sb][tt % 2]
        e_lnb = ev_lnap[tt]
        e_trg = [None, None]
        for g in range(2):
            wts = _later(e_lnb, ev_lntr.get(tt - 1),
                         ev_ini if tt == 0 else None)
            for j in range(8):
                e_trg[g] = tr_ln(
                    TRP[:, g * 1024 + j * 128:g * 1024 + (j + 1) * 128],
                    ln_bf[:, (8 * g + j) * 128:(8 * g + j + 1) * 128],
                    waits=wts, inc="pe" if j == 7 else None)
                wts = ()
        e_ev1 = dve(lambda e: e.tensor_copy(lnT_sb[:, 0:1024],
                                            TRP[:, 0:1024]),
                    waits=_later(e_trg[0]))
        e_ev2 = dve(lambda e: e.tensor_copy(lnT_sb[:, 1024:2048],
                                            TRP[:, 1024:2048]),
                    waits=_later(e_trg[1]))
        ev_lntr[tt] = e_ev2
        if tt + 1 < TT:
            emit_chain(tt + 1)
        flush_rope()
        segw = [512, 512, 512, 512, 256]
        ev_seg = [None] * 5
        for seg in range(5):
            pb = SEG[:, (seg % 2) * 512:(seg % 2) * 512 + segw[seg]]
            guard = {0: ev_half.get((tt - 1, 0)), 1: ev_half.get((tt - 1, 1)),
                     2: ev_half.get((tt, 0)), 3: ev_half.get((tt, 1)),
                     4: ev_half.get((tt, 0))}[seg]
            for ec in range(16):
                wts = _later(e_ev1, e_ev2, ev_wA, guard) if ec == 0 else ()
                ev_seg[seg] = mm(
                    pb, lnT_sb[:, ec * 128:(ec + 1) * 128],
                    wA_sb[:, ec * 2304 + seg * 512:
                          ec * 2304 + seg * 512 + segw[seg]],
                    ec == 0, ec == 15, waits=wts,
                    inc="pe" if ec == 15 else None)
            if seg == 1:
                flush_qktr()
            if seg < 4:
                e_cp = dve(lambda e, seg=seg, qkb=qkb: e.tensor_copy(
                    qkb[:, seg * 512:(seg + 1) * 512],
                    SEG[:, (seg % 2) * 512:(seg % 2) * 512 + 512]),
                    waits=_later(ev_seg[seg], ev_trqk.get(tt - 2)))
                ev_half[(tt, seg % 2)] = e_cp
            else:
                e_cpk = dve(lambda e, qkb=qkb: e.tensor_copy(
                    qkb[:, 2048:2176], SEG[:, 0:128]),
                    waits=_later(ev_seg[4], ev_trqk.get(tt - 2)))
                e_vc = dve(lambda e, tt=tt: e.tensor_copy(
                    V_sb[:, tt * 128:(tt + 1) * 128], SEG[:, 128:256]),
                    waits=_later(ev_seg[4]))
                ev_half[(tt, 0)] = e_vc
                ev_vc[tt] = e_vc
        def rope_emit(tt=tt, qkb=qkb, ev_h0=ev_half[(tt, 0)],
                      ev_h1=ev_half[(tt, 1)]):
            qk3 = qkb[:, :].rearrange("p (h d) -> p h d", d=128)
            A1 = qk3[:, :, 0:64]
            A2 = qk3[:, :, 64:128]
            cosb = cos_sb[:, tt * 64:(tt + 1) * 64].unsqueeze(1) \
                .broadcast_to((128, 17, 64))
            sinb = sin_sb[:, tt * 64:(tt + 1) * 64].unsqueeze(1) \
                .broadcast_to((128, 17, 64))
            rtv = [r[:, :].rearrange("p (h d) -> p h d", d=64) for r in rt]
            e_r1 = pool(lambda e, a=A1, b=cosb: e.tensor_tensor(
                rtv[0], a, b, ALU.mult),
                waits=_later(ev_h0, ev_h1, ev_ini))
            e_r2 = pool(lambda e, a=A2, b=sinb: e.tensor_tensor(
                rtv[1], a, b, ALU.mult))
            e_r3 = pool(lambda e, a=A1, b=sinb: e.tensor_tensor(
                rtv[2], a, b, ALU.mult))
            e_r4 = pool(lambda e, a=A2, b=cosb: e.tensor_tensor(
                rtv[3], a, b, ALU.mult))
            e_o1 = pool(lambda e, a=A1: e.tensor_tensor(
                a, rtv[0], rtv[1], ALU.subtract),
                waits=_later(e_r1, e_r2, e_r3, e_r4))
            e_o2 = pool(lambda e, a=A2: e.tensor_tensor(
                a, rtv[2], rtv[3], ALU.add), waits=_later(e_o1))

            rope_o2[tt] = e_o2

        def qktr(tt=tt, qkb=qkb):
            e_o2 = rope_o2[tt]
            gloc = tt % 4
            half = (tt // 4) % 2
            qgv = qstg[:, half * 8192:(half + 1) * 8192] \
                .rearrange("p (h c) -> p h c", c=512)
            prev = ev_eqx.get(tt - 1)
            gev = [None] * 3
            wts = _later(e_o2, prev[1] if prev else None,
                         prev[2] if prev else None)
            for j in range(8):
                gev[0] = tr_qk(TRP2[:, j * 128:(j + 1) * 128],
                            qkb[:, j * 128:(j + 1) * 128],
                            waits=wts, inc="pe" if j == 7 else None)
                wts = ()
            e_eq0 = dve(lambda e, gloc=gloc, qgv=qgv: e.tensor_copy(
                qgv[:, 0:8, gloc * 128:(gloc + 1) * 128],
                TRP2[:, 0:1024].rearrange("p (h c) -> p h c", c=128)),
                waits=_later(gev[0], ev_qfl.get(tt // 4 - 2)))
            wts = ()
            for j in range(8):
                gev[1] = tr_qk(TRP2[:, 1024 + j * 128:1024 + (j + 1) * 128],
                            qkb[:, (8 + j) * 128:(9 + j) * 128],
                            waits=wts, inc="pe" if j == 7 else None)
            e_eq1 = dve(lambda e, gloc=gloc, qgv=qgv: e.tensor_copy(
                qgv[:, 8:16, gloc * 128:(gloc + 1) * 128],
                TRP2[:, 1024:2048].rearrange("p (h c) -> p h c", c=128)),
                waits=_later(gev[1]))
            gev[2] = tr_qk(TRP2[:, 0:128], qkb[:, 16 * 128:17 * 128],
                        waits=_later(e_eq0), inc="pe")
            e_ek = dve(lambda e, tt=tt: e.tensor_copy(
                K_sb[:, tt * 128:(tt + 1) * 128], TRP2[:, 0:128]),
                waits=_later(gev[2]))
            ev_kc[tt] = e_ek
            ev_trqk[tt] = gev[2]
            ev_eqx[tt] = (e_eq0, e_eq1, e_ek)
            if tt % 4 == 3:
                g4 = tt // 4
                evs = _later(e_eq0, e_eq1, e_ek)
                for h in range(NH):
                    ev_qfl[g4] = dma(
                        "scalar", q_scr[h, :, g4 * 512:(g4 + 1) * 512],
                        qstg[:, half * 8192 + h * 512:
                             half * 8192 + (h + 1) * 512],
                        waits=evs if h == 0 else (), sem="sqf",
                        serial=False)
        pend_rope[0] = rope_emit
        pend_qktr[0] = qktr
        emit_casts(2, waits=_later(ldx[tt]))
    flush_rope()
    flush_qktr()
    emit_casts(1000)
    ev_wo = cast_ev["sgW"]
    ev_p1c = cast_ev["sgP1"]
    ev_p2c = cast_ev["sgP2"]

    PEA = ("pe", S.counts.get("pe", 0))
    DVEA = ("dve", S.counts.get("dve", 0))
    ACTA = ("act", S.counts.get("act", 0))
    SQFA = ("sqf", S.counts.get("sqf", 0))
    G0 = (PEA, DVEA, ACTA, SQFA)

    ev_exp = {}
    ev_sc2 = {}
    ring = {s: [G0] for s in range(8)}
    ev_qsld = {}
    qs_guard = {0: G0, 1: G0}
    ev_avevac = {}
    ev_dmm = {}
    ev_rc = {}
    ev_aT = {}
    ev_mm1 = {}
    ev_gelu = {}
    ev_cp1 = {}
    ev_p1ld = {}
    p1_guard = {0: G0, 1: G0}
    wr_guard = {0: G0, 1: G0}
    wr_i = [0]
    ev_mm2ep = {}
    ev_ysto = {}
    ev_opj = {}
    ev_mm2last = {}
    ev_x1w = {}
    ev_stgdone = {}
    ev_xld = {}
    x_guard = {0: G0, 1: G0}
    ln2slot_guard = {0: G0, 1: G0}
    trc_guard = {0: G0, 1: G0}
    att_tail = {}
    b2c_ev = [None]

    pend_gelu = [None]
    pend_tail = [None, None]

    def flush_gelu():
        if pend_gelu[0] is not None:
            pend_gelu[0]()
            pend_gelu[0] = None

    def flush_tail(i=None):
        for j in ((0, 1) if i is None else (i,)):
            if pend_tail[j] is not None:
                pend_tail[j]()
                pend_tail[j] = None

    def emit_av(c, h, p, s_adds):
        h_g = c * NH + h
        g = h_g * NP + p
        slot = g % 8
        evav = None
        for kk in (2 * p, 2 * p + 1):
            w = ()
            if kk == 2 * p:
                w = _later(ev_exp[g],
                           ev_avevac.get(h_g - 1) if p == 0 else None,
                           G0 if h_g == 0 and p == 0 else None)
            evav = mm_avv(AVp, V_sb[:, kk * 128:(kk + 1) * 128],
                      E_sb[:, slot * 1024 + (kk - 2 * p) * 512:
                           slot * 1024 + (kk - 2 * p + 1) * 512],
                      p == 0 and kk == 2 * p,
                      p == NP - 1 and kk == 2 * p + 1,
                      waits=w, inc="pe")
        ring[slot] = [evav, s_adds[p]]
        return evav

    def emit_head(c, h, take_fill):
        h_g = c * NH + h
        qpar = h % 2
        Sb = SS[h % 2]
        if (c, h) not in ev_qsld:
            ev_qsld[(c, h)] = dma(
                "sync", Qs[qpar][:, :], q_scr[h, :, c * 512:(c + 1) * 512],
                waits=_later(qs_guard[qpar],
                             G0 if c == 0 and h <= 1 else None),
                sem=f"sq{qpar}")
        nx_c, nx_h = (c, h + 1) if h + 1 < NH else (c + 1, 0)
        if nx_c < NCH and (nx_c, nx_h) not in ev_qsld:
            ev_qsld[(nx_c, nx_h)] = dma(
                "sync", Qs[nx_h % 2][:, :],
                q_scr[nx_h, :, nx_c * 512:(nx_c + 1) * 512],
                waits=_later(qs_guard[nx_h % 2],
                             G0 if c == 0 and h == 0 else None),
                sem=f"sq{nx_h % 2}")
        flush_tail(0)
        s_adds = {}
        evav = [None]
        for p in range(NP):
            g = h_g * NP + p
            slot = g % 8
            pp = PP[g % 2]
            w0 = _later(ev_qsld[(c, h)], ev_exp.get(g - 2),
                        ev_stgdone.get(c - 1) if (h == 0 and p == 0)
                        else None,
                        G0 if (c == 0 and h == 0 and p == 0) else None)
            mm_sc(pp[:, 0:512], K_sb[:, (2 * p) * 128:(2 * p + 1) * 128],
                  Qs[qpar][:, :], True, True, waits=w0)
            ev_sc2[g] = mm_sc(pp[:, 512:1024],
                           K_sb[:, (2 * p + 1) * 128:(2 * p + 2) * 128],
                           Qs[qpar][:, :], True, True, inc="pe")
            if p == 2:
                flush_tail(1)
            if p >= 2:
                evav[0] = emit_av(c, h, p - 2, s_adds)
                take_fill(4)
            ev_exp[g] = act(
                lambda e, pp=pp, slot=slot: e.activation(
                    E_sb[:, slot * 1024:(slot + 1) * 1024], pp[:, :],
                    AF.Exp, scale=ISQ),
                waits=_later(ev_sc2[g], *ring[slot]))
            if p == 0:
                s_adds[p] = dve(lambda e, slot=slot, Sb=Sb: e.tensor_tensor(
                    Sb[:, :], E_sb[:, slot * 1024:slot * 1024 + 512],
                    E_sb[:, slot * 1024 + 512:(slot + 1) * 1024], ALU.add),
                    waits=_later(ev_exp[g], ev_dmm.get(h_g - 2)))
            else:
                dve(lambda e, slot=slot: e.tensor_tensor(
                    tmpP[:, :], E_sb[:, slot * 1024:slot * 1024 + 512],
                    E_sb[:, slot * 1024 + 512:(slot + 1) * 1024], ALU.add),
                    waits=_later(ev_exp[g]))
                s_adds[p] = dve(lambda e, Sb=Sb: e.tensor_tensor(
                    Sb[:, :], Sb[:, :], tmpP[:, :], ALU.add))
        evav[0] = emit_av(c, h, NP - 2, s_adds)
        take_fill(4)
        evav[0] = emit_av(c, h, NP - 1, s_adds)
        take_fill(4)
        qs_guard[qpar] = ev_sc2[h_g * NP + NP - 1]

        def tail0(h_g=h_g, evav=evav[0]):
            ev_avevac[h_g] = dve(lambda e: e.tensor_copy(
                avs[:, :], AVp[:, :]), waits=_later(evav))

        def tail1(h_g=h_g, h=h, c=c, Sb=Sb, s_last=s_adds[NP - 1]):
            ev_dmm[h_g] = mm(B7[:, :], ones128_sb[:, :], Sb[:, :],
                             True, True,
                             waits=_later(s_last, ev_rc.get(h_g - 1),
                                          b2c_ev[0] if h == 0 else None,
                                          G0 if h_g == 0 else None),
                             inc="pe")
            ev_rc[h_g] = dve(lambda e: e.reciprocal(rD_sb[:, :], B7[:, :]),
                             waits=_later(ev_dmm[h_g]))
            ev_aT[h_g] = dve(lambda e, h=h: e.tensor_tensor(
                aT_sb[:, h * 512:(h + 1) * 512], avs[:, :], rD_sb[:, :],
                ALU.mult),
                waits=_later(ev_opj.get(c - 1) if h == 0 else None,
                             G0 if h_g == 0 else None))
        pend_tail[0] = tail0
        pend_tail[1] = tail1

    def filler_stream(cf):
        for ht in range(NHT):
            pb = ht % 2
            for ec in range(16):
                def emit_one(ht=ht, ec=ec, pb=pb):
                    if ec == 0:
                        flush_gelu()
                        for hh in (ht, ht + 1):
                            if hh < NHT and (cf, hh) not in ev_p1ld:
                                ev_p1ld[(cf, hh)] = dma(
                                    ("sync", "gpsimd")[hh % 2],
                                    p1s[hh % 2][:, :], p1_bf[hh],
                                    waits=_later(p1_guard[hh % 2],
                                                 ev_p1c if cf == 0 and
                                                 hh <= 1 else None),
                                    sem=f"sp1{hh % 2}")
                        w0 = _later(ev_p1ld[(cf, ht)],
                                    ev_cp1.get((cf, ht - 2)),
                                    ev_stgdone.get(cf),
                                    G0 if cf == 0 and ht <= 1 else None)
                    else:
                        w0 = ()
                    ev = mm_f1(M1[pb], p1s[pb][:, ec * 128:(ec + 1) * 128],
                            stg[:, ec * 512:(ec + 1) * 512],
                            ec == 0, ec == 15, waits=w0,
                            inc="pe" if ec == 15 else None)
                    if ec == 15:
                        ev_mm1[(cf, ht)] = ev
                        p1_guard[pb] = ev
                        ev_cp1[(cf, ht)] = dve(
                            lambda e, ht=ht, pb=pb: e.tensor_copy(
                                hT_sb[:, ht * 512:(ht + 1) * 512], M1[pb]),
                            waits=_later(ev, G0 if cf == 0 else None))
                yield emit_one

    def emit_mm2(cf):
        PSs = [PA, PB]
        tail = att_tail.get(cf + 1, att_tail[cf])
        stop_ev = {}
        for ht in range(NHT):
            ev_gelu[(cf, ht)] = act(
                lambda e, ht=ht: e.activation(
                    hT_sb[:, ht * 512:(ht + 1) * 512],
                    hT_sb[:, ht * 512:(ht + 1) * 512], AF.Gelu,
                    bias=b1r_sb[:, ht:ht + 1]),
                waits=_later(ev_cp1[(cf, ht)], ev_ini))
        W6 = [wr[0], wr[1]] + [stg[:, i * 2048:(i + 1) * 2048]
                               for i in range(4)]
        NW = 6
        ldev = {}
        grp_last = {}

        def issue_load(j):
            hcj = j % NHT
            sl = j % NW
            w = _later(grp_last.get(j - NW),
                       wr_guard[sl] if (j < NW and sl < 2) else None,
                       ev_mm1[(cf, NHT - 1)] if (j < NW and sl >= 2)
                       else None,
                       ev_p2c if cf == 0 and j < NW else None)
            ldev[j] = dma(("sync", "gpsimd")[sl % 2], W6[sl][:, :],
                          p2_bf[hcj * 128:(hcj + 1) * 128, :], waits=w,
                          sem=f"swr{sl % 2}")
        for j in range(NW):
            issue_load(j)
        for pr in range(2):
            tls = (2, 3) if pr == 0 else (0, 1)
            if pr == 1:
                for tl in (0, 1):
                    ev_x1r = dma(
                        "gpsimd", x1t[tl % 2][:, :],
                        x1_scr[(cf * 4 + tl) * 128:(cf * 4 + tl + 1) * 128,
                               :],
                        waits=_later(ev_x1w[(cf, tl)],
                                     ev_mm2ep[(cf, tl + 2)]),
                        sem=f"sx1r{tl % 2}")
                    ev_x1w[(cf, tl, "r")] = ev_x1r
            for hc in range(NHT):
                j = pr * NHT + hc
                sl = j % NW
                ev_w = ldev[j]
                last = None
                for ti, tl in enumerate(tls):
                    ps = PSs[ti]
                    for s4 in range(4):
                        w = ()
                        if hc == 0 and s4 == 0:
                            if pr == 0:
                                w = _later(ev_w, ev_gelu[(cf, hc)],
                                           tail[0], tail[1],
                                           ev_stgdone[cf],
                                           ev_gelu[(cf, NHT - 1)]
                                           if ti == 1 else None)
                            else:
                                w = _later(ev_w, ev_mm2ep[(cf, tls[ti] + 2)])
                        elif s4 == 0:
                            w = _later(ev_w, ev_gelu[(cf, hc)])
                        last = mm_f2(ps[:, s4 * 512:(s4 + 1) * 512],
                                     hT_sb[:, hc * 512 + tl * 128:
                                     hc * 512 + (tl + 1) * 128],
                                     W6[sl][:, s4 * 512:(s4 + 1) * 512],
                                     hc == 0, hc == NHT - 1, waits=w,
                                     inc="pe" if (s4 == 3 and
                                                  (ti == 1 or
                                                   hc == NHT - 1))
                                     else None)
                        if s4 == 3 and hc == NHT - 1:
                            stop_ev[tl] = last
                grp_last[j] = last
                if j + NW < 2 * NHT:
                    issue_load(j + NW)
            for ti, tl in enumerate(tls):
                yv = y_stage[:, ti * 2048:(ti + 1) * 2048]
                wts = _later(stop_ev[tl],
                             ev_x1w.get((cf, tl, "r")),
                             ev_ysto.get((cf, tl + 2)),
                             ev_ysto.get((cf - 1, tl - 2)) if pr == 0
                             else None,
                             tail[0] if pr == 0 else None)
                ev_mm2ep[(cf, tl)] = dve(
                    lambda e, yv=yv, ps=PSs[ti], tl=tl: e.tensor_tensor(
                        yv, ps[:, :], x1t[tl % 2][:, :], ALU.add),
                    waits=wts)
                ev_ysto[(cf, tl)] = dma(
                    "scalar",
                    y_d[(cf * 4 + tl) * 128:(cf * 4 + tl + 1) * 128, :],
                    yv, waits=_later(ev_mm2ep[(cf, tl)]), sem="ssy",
                    serial=False)
        wr_guard[0] = grp_last[126]
        wr_guard[1] = grp_last[127]
        ev_mm2last[cf] = stop_ev[1]
        for s in range(8):
            ring[s] = ring[s] + [ev_ysto[(cf, 2 if s < 4 else 3)],
                                 ev_ysto[(cf, 0 if s < 4 else 1)]]

    def emit_outproj(c):
        PSs = [PA, PB]
        if b2c_ev[0] is None:
            e_l = dma("sync", b2b_sb[0:1, :], b2row_d[:, :],
                      waits=_later(G0), sem="sini", serial=False)
            e_c = None
            for s4 in range(4):
                e_m = mm(B7[:, :], onesr_sb[0:1, :],
                         b2b_sb[0:1, s4 * 512:(s4 + 1) * 512], True, True,
                         waits=_later(e_l, ev_aT[c * NH + NH - 1],
                                      e_c), inc="pe")
                e_c = dve(lambda e, s4=s4: e.tensor_copy(
                    b2b_sb[:, s4 * 512:(s4 + 1) * 512], B7[:, :]),
                    waits=_later(e_m))
            b2c_ev[0] = e_c
        for tl in (0, 1):
            ev_xld[(c, tl)] = dma(
                "gpsimd", x_sb[tl % 2][:, :],
                x_d[(c * 4 + tl) * 128:(c * 4 + tl + 1) * 128, :],
                waits=_later(x_guard[tl % 2], G0 if c == 0 else None),
                sem=f"sx{tl % 2}")
        stop_ev = {}
        ev_x1add = {}
        ev_l2ap = {}
        ev_b2ad = {}
        e_evc = {}
        stgv = stg[:, :].rearrange("p (e c) -> p e c", c=512)

        W4 = [wr[0], wr[1], p1s[0], p1s[1]]
        W4sem = ["swr0", "swr1", "sp10", "sp11"]
        ldev = {}
        grp_last = {}

        def issue_load(j):
            dhj = j % 16
            sl = j % 4
            w = _later(grp_last.get(j - 4),
                       (wr_guard[sl] if sl < 2 else
                        ev_mm1.get((c - 1, NHT - 1))) if j < 4 else None,
                       G0 if c == 0 and j < 4 else None,
                       ev_wo if c == 0 and j < 4 else None)
            ldev[j] = dma(("sync", "gpsimd")[sl % 2], W4[sl][:, :],
                          woT_bf[dhj * 128:(dhj + 1) * 128, :], waits=w,
                          sem=W4sem[sl])
        for j in range(4):
            issue_load(j)

        def emit_pr(pr):
            tls = (pr * 2, pr * 2 + 1)
            for dh in range(16):
                j = pr * 16 + dh
                sl = j % 4
                ev_w = ldev[j]
                last = None
                for ti, tl in enumerate(tls):
                    ps = PSs[ti]
                    for s4 in range(4):
                        w = ()
                        if dh == 0 and s4 == 0:
                            if pr == 0:
                                w = _later(ev_w, ev_aT[c * NH + NH - 1],
                                           att_tail[c][0],
                                           ev_mm2ep.get((c - 1, 1)),
                                           b2c_ev[0])
                            else:
                                w = _later(ev_w, ev_x1add[tls[ti] - 2])
                        elif s4 == 0:
                            w = _later(ev_w)
                        last = mm_op(ps[:, s4 * 512:(s4 + 1) * 512],
                                     aT_sb[:, dh * 512 + tl * 128:
                                     dh * 512 + (tl + 1) * 128],
                                     W4[sl][:, s4 * 512:(s4 + 1) * 512],
                                     dh == 0, dh == 15, waits=w,
                                     inc="pe" if (s4 == 3 and
                                                  (ti == 1 or dh == 15))
                                     else None)
                        if s4 == 3 and dh == 15:
                            stop_ev[tl] = last
                grp_last[j] = last
                if j + 4 < 32:
                    issue_load(j + 4)
            ev_opj[c] = stop_ev[tls[1]]

        def x1_part(tl):
            xb = tl % 2
            ev_x1add[tl] = dve(
                lambda e, xb=xb, ps=PSs[tl % 2]: e.tensor_tensor(
                    x1t[xb][:, :], ps[:, :], x_sb[xb][:, :], ALU.add),
                waits=_later(stop_ev[tl], ev_xld[(c, tl)],
                             ev_x1w.get((c, tl - 2)),
                             ev_b2ad.get(tl - 2),
                             G0 if c == 0 else None))
            x_guard[xb] = ev_x1add[tl]
            if tl + 2 <= 3:
                ev_xld[(c, tl + 2)] = dma(
                    "gpsimd", x_sb[xb][:, :],
                    x_d[(c * 4 + tl + 2) * 128:(c * 4 + tl + 3) * 128, :],
                    waits=_later(ev_x1add[tl]), sem=f"sx{xb}")

        def ln2_part(tl):
            xb = tl % 2
            so = xb * 8
            wx = _later(ev_x1add[tl], ln2slot_guard[xb],
                        G0 if c == 0 else None)
            e_l2, _ = ln_chain(
                x1t[xb][:, :], so, wx,
                ln2b4[:, xb * 2048:(xb + 1) * 2048],
                ln2b4[:, xb * 2048:(xb + 1) * 2048])
            ev_l2ap[tl] = e_l2
            ev_b2ad[tl] = dve(
                lambda e, xb=xb: e.tensor_tensor(
                    x1t[xb][:, :], x1t[xb][:, :], b2b_sb[:, :], ALU.add),
                waits=_later(e_l2, b2c_ev[0]))
            if tl <= 1:
                ev_x1w[(c, tl)] = dma(
                    "gpsimd",
                    x1_scr[(c * 4 + tl) * 128:(c * 4 + tl + 1) * 128, :],
                    x1t[xb][:, :], waits=_later(ev_b2ad[tl]),
                    sem="sx1w", serial=False)

        def tr_part(tl):
            xb = tl % 2
            buf = TRC[xb]
            wts = _later(ev_l2ap[tl], ev_x1add[2 + xb], trc_guard[xb])
            e_t = None
            for j in range(16):
                e_t = tr(buf[:, j * 128:(j + 1) * 128],
                         ln2b4[:, xb * 2048 + j * 128:
                               xb * 2048 + (j + 1) * 128],
                         waits=wts, inc="pe" if j == 15 else None)
                wts = ()
            ln2slot_guard[xb] = e_t
            e_evc[tl] = dve(lambda e, tl=tl, buf=buf: e.tensor_copy(
                stgv[:, :, tl * 128:(tl + 1) * 128],
                buf[:, :].rearrange("p (e c) -> p e c", c=128)),
                waits=_later(e_t, ev_mm1.get((c - 1, NHT - 1)),
                             ev_mm2last.get(c - 1)))
            trc_guard[xb] = e_evc[tl]

        emit_pr(0)
        x1_part(0)
        x1_part(1)
        ln2_part(0)
        ln2_part(1)
        emit_pr(1)
        x1_part(2)
        x1_part(3)
        tr_part(0)
        tr_part(1)
        ln2_part(2)
        ln2_part(3)
        tr_part(2)
        tr_part(3)
        ev_stgdone[c] = e_evc[3]
        wr_guard[0] = grp_last[28]
        wr_guard[1] = grp_last[29]
        p1_guard[0] = grp_last[30]
        p1_guard[1] = grp_last[31]

    for k in range(NCH + 1):
        cA, cF = k, k - 1
        fs = filler_stream(cF) if cF >= 0 else iter(())

        def take_fill(n, fs=fs):
            for _ in range(n):
                try:
                    next(fs)()
                except StopIteration:
                    break
        if cA < NCH:
            for h in range(NH):
                emit_head(cA, h, take_fill)
            flush_tail()
            att_tail[cA] = (ev_exp[(cA * NH + NH - 1) * NP + NP - 1],
                            ev_aT[cA * NH + NH - 1])
        for f in fs:
            f()
        flush_gelu()
        if cF >= 0:
            emit_mm2(cF)
        if cA < NCH:
            emit_outproj(cA)

    dma_sems = [k for k in S.counts if k not in ("pe", "act", "dve")]
    finals = tuple((k, S.counts[k]) for k in dma_sems)
    S.emit("scalar", lambda e: e.nop(),
           waits=tuple((k, v) for k, v in finals
                       if k in ("ssy", "sx1w", "sqf")))
    S.emit("sync", lambda e: e.nop(), waits=finals)

    with nc.Block() as block:
        S.replay(block)
    nc.finalize()
    return nc


def _prep(w_attn, w_out, proj1, proj1_bias, proj2, proj2_bias, T):
    TT = T // 128
    d = {}
    d["wqkvT"] = np.ascontiguousarray(w_attn.T)
    d["woT"] = np.ascontiguousarray(w_out.T)
    d["p1blk"] = np.ascontiguousarray(
        proj1.reshape(NHT, 128, 16, 128).transpose(0, 3, 2, 1)
        .reshape(NHT, 128, 2048))
    d["p2T"] = np.ascontiguousarray(proj2.T)
    d["b1r"] = np.ascontiguousarray(proj1_bias.reshape(NHT, 128).T)
    d["b2row"] = np.ascontiguousarray(proj2_bias.reshape(1, D))
    j = np.arange(0, HD, 2, dtype=np.float64)
    pos = (10000.0 ** ((-2.0 * j - 1.0) / HD)).astype(np.float32)
    ang = np.arange(T, dtype=np.float32)[:, None] * pos[None, :]
    cos = np.cos(ang).astype(np.float32)
    sin = np.sin(ang).astype(np.float32)
    d["cosA"] = np.ascontiguousarray(
        cos.reshape(TT, 128, 64).transpose(1, 0, 2).reshape(128, TT * 64))
    d["sinA"] = np.ascontiguousarray(
        sin.reshape(TT, 128, 64).transpose(1, 0, 2).reshape(128, TT * 64))
    d["ident"] = np.eye(128, dtype=ml_dtypes.bfloat16)
    d["onescol"] = np.ones((128, 1), dtype=np.float32)
    d["ones128"] = np.ones((128, 128), dtype=np.float32)
    d["onesr"] = np.ones((1, 128), dtype=np.float32)
    return d


def kernel(L0_attn_x, L0_attn_w_attn, L0_attn_w_out, L0_ffn_proj1,
           L0_ffn_proj1_bias, L0_ffn_proj2, L0_ffn_proj2_bias,
           L0_ffn_gate_w=None, **_unused):
    x = np.asarray(L0_attn_x, dtype=np.float32)
    B, T, _ = x.shape
    shared = _prep(np.asarray(L0_attn_w_attn, np.float32),
                   np.asarray(L0_attn_w_out, np.float32),
                   np.asarray(L0_ffn_proj1, np.float32),
                   np.asarray(L0_ffn_proj1_bias, np.float32),
                   np.asarray(L0_ffn_proj2, np.float32),
                   np.asarray(L0_ffn_proj2_bias, np.float32), T)
    nc = build_program(T)
    in_maps = []
    for b in range(B):
        m = dict(shared)
        m["x"] = np.ascontiguousarray(x[b])
        in_maps.append(m)
    res = run_bass_kernel_spmd(nc, in_maps, core_ids=list(range(B)))
    out = np.stack([np.asarray(r["y"], dtype=np.float32)
                    for r in res.results], axis=0)
    return out


# version 19
# speedup vs baseline: 1.1149x; 1.0867x over previous
import numpy as np
import ml_dtypes
import concourse.bass as bass
import concourse.mybir as mybir
from concourse.bass_utils import run_bass_kernel_spmd

F32 = mybir.dt.float32
BF16 = mybir.dt.bfloat16
AF = mybir.ActivationFunctionType
ALU = mybir.AluOpType
AX = mybir.AxisListType

D = 2048
NH = 16
HD = 128
HID = 8192
NHT = HID // 128
EPS = 1e-5
ISQ = float(1.0 / np.sqrt(128.0))

ENGINES = ("tensor", "scalar", "vector", "sync", "gpsimd")


class Sched:

    def __init__(self, nc):
        self.nc = nc
        self.items = {e: [] for e in ENGINES}
        self.sems = {}
        self.counts = {}

    def _sem(self, name):
        if name not in self.sems:
            self.sems[name] = self.nc.alloc_semaphore(name=name)
            self.counts[name] = 0
        return self.sems[name]

    def emit(self, eng, fn, waits=(), inc=None, inc_by=1):
        ev = None
        if inc is not None:
            self._sem(inc)
            self.counts[inc] += inc_by
            ev = (inc, self.counts[inc])
        ws = tuple((s, v) for (s, v) in waits if s is not None and v > 0)
        for s, _ in ws:
            self._sem(s)
        self.items[eng].append((fn, ws, inc, inc_by))
        return ev

    def replay(self, block):
        for ename in ENGINES:
            seq = self.items[ename]
            if not seq:
                continue

            def mk(seq):
                def body(eng):
                    wm = {}
                    for fn, waits, inc, inc_by in seq:
                        for (sname, val) in waits:
                            if wm.get(sname, 0) >= val:
                                continue
                            eng.wait_ge(self.sems[sname], val)
                            wm[sname] = val
                        ins = fn(eng)
                        if inc is not None:
                            ins.then_inc(self.sems[inc], inc_by)
                return body
            getattr(block, ename)(mk(seq))


def _later(*evs):
    best = {}

    def add(ev):
        if ev is None:
            return
        if isinstance(ev, tuple) and len(ev) == 2 and isinstance(ev[0], str):
            s, v = ev
            if best.get(s, -1) < v:
                best[s] = v
        else:
            for e in ev:
                add(e)
    for ev in evs:
        add(ev)
    return tuple(best.items())


def build_program(T):
    TT = T // 128
    KT = T // 128
    NP = KT // 2
    NCH = T // 512
    assert TT % 4 == 0

    nc = bass.Bass("TRN2", target_bir_lowering=False)

    x_d = nc.dram_tensor("x", [T, D], F32, kind="ExternalInput")
    wqkvT_d = nc.dram_tensor("wqkvT", [D, 2304], F32, kind="ExternalInput")
    woT_d = nc.dram_tensor("woT", [D, D], F32, kind="ExternalInput")
    p1blk_d = nc.dram_tensor("p1blk", [NHT, 128, 2048], F32,
                             kind="ExternalInput")
    p2T_d = nc.dram_tensor("p2T", [HID, D], F32, kind="ExternalInput")
    b1r_d = nc.dram_tensor("b1r", [128, NHT], F32, kind="ExternalInput")
    b2row_d = nc.dram_tensor("b2row", [1, D], F32, kind="ExternalInput")
    cosA_d = nc.dram_tensor("cosA", [128, TT * 64], F32, kind="ExternalInput")
    sinA_d = nc.dram_tensor("sinA", [128, TT * 64], F32, kind="ExternalInput")
    ident_d = nc.dram_tensor("ident", [128, 128], BF16, kind="ExternalInput")
    onescol_d = nc.dram_tensor("onescol", [128, 1], F32, kind="ExternalInput")
    onesr_d = nc.dram_tensor("onesr", [1, 128], F32, kind="ExternalInput")
    ones128_d = nc.dram_tensor("ones128", [128, 128], F32,
                               kind="ExternalInput")
    y_d = nc.dram_tensor("y", [T, D], F32, kind="ExternalOutput")
    woT_bf = nc.dram_tensor("woT_bf", [D, D], BF16, kind="Internal")
    p1_bf = nc.dram_tensor("p1_bf", [NHT, 128, 2048], BF16, kind="Internal")
    p2_bf = nc.dram_tensor("p2_bf", [HID, D], BF16, kind="Internal")
    q_scr = nc.dram_tensor("q_scr", [NH, 128, T], BF16, kind="Internal")
    x1_scr = nc.dram_tensor("x1_scr", [T, D], F32, kind="Internal")

    ORIGIN = 20480
    LIMIT = 229376 - 128
    off = [ORIGIN]

    def sb(name, cols, dt, at=None, parts=128):
        nbytes = cols * (2 if dt == BF16 else 4)
        if at is None:
            at = off[0]
            off[0] = at + ((nbytes + 31) // 32) * 32
        return nc.alloc_sbuf_tensor_at(name, [parts, cols], dt, offset=at)

    K_sb = sb("K_sb", T, BF16)
    V_sb = sb("V_sb", T, BF16)
    x_sb = [sb("x0", D, F32), sb("x1b", D, F32)]
    stats = sb("stats", 32, F32)
    ident_sb = sb("ident_s", 128, BF16)
    onescol_sb = sb("onescol_s", 1, F32)
    b1r_sb = sb("b1r_s", NHT, F32)
    onesr_sb = sb("onesr_s", 128, F32, parts=1)
    ones128_sb = sb("ones128_s", 128, F32)

    AR0 = off[0]
    E_sb = sb("E_sb", 8 * 1024, BF16, at=AR0)
    y_stage = sb("y_stage", 2 * 2048, F32, at=AR0)
    aT_sb = sb("aT_sb", NH * 512, BF16, at=AR0 + 16384)
    Qs = [sb("Qs0", 512, BF16, at=AR0 + 32768),
          sb("Qs1", 512, BF16, at=AR0 + 33792)]
    hT_sb = sb("hT", NHT * 512, BF16, at=AR0 + 34816)
    stg = sb("stg", 16 * 512, BF16, at=AR0 + 100352)
    ln2b4 = sb("ln2b4", 2 * 2048, BF16, at=AR0 + 116736)
    avs = sb("avs", 512, F32, at=AR0 + 116736)
    p1s = [sb("p1a", 2048, BF16, at=AR0 + 124928),
           sb("p1b", 2048, BF16, at=AR0 + 129024)]
    wr = [sb("wr0", 2048, BF16, at=AR0 + 133120),
          sb("wr1", 2048, BF16, at=AR0 + 137216)]
    x1t = [sb("x1t0", 2048, F32, at=AR0 + 141312),
           sb("x1t1", 2048, F32, at=AR0 + 149504)]
    b2b_sb = sb("b2b", 2048, F32, at=AR0 + 157696)
    SS = [sb("S_sb", 512, F32, at=AR0 + 165888),
          sb("S2_sb", 512, F32, at=AR0 + 167936)]
    tmpP = sb("tmpP", 512, F32, at=AR0 + 169984)
    rD_sb = sb("rD_sb", 512, F32, at=AR0 + 172032)
    CH_END = AR0 + 174080
    wA_sb = sb("wA", 16 * 2304, BF16, at=AR0)
    cos_sb = sb("cosA_s", TT * 64, F32, at=AR0 + 73728)
    sin_sb = sb("sinA_s", TT * 64, F32, at=AR0 + 81920)
    rt = [sb(f"rt{i}", 1088, BF16, at=AR0 + 90112 + i * 2176)
          for i in range(4)]
    qk_sb = sb("qk", 2176, BF16, at=AR0 + 98816)
    lnT_sb = sb("lnT", D, BF16, at=AR0 + 103168)
    ln_bf = sb("ln_bf", D, BF16, at=AR0 + 107264)
    qstg = sb("qstg", 2 * 8192, BF16, at=AR0 + 111360)
    atmpA = sb("atmpA", D, BF16, at=AR0 + 144128)
    xA = [x_sb[0], x_sb[1],
          sb("xA2", D, F32, at=AR0 + 148224),
          sb("xA3", D, F32, at=AR0 + 156416)]
    qk2_sb = sb("qk2", 2176, BF16, at=AR0 + 164608)
    A_END = AR0 + 168960
    assert max(CH_END, A_END) <= LIMIT, f"sbuf overflow {CH_END} > {LIMIT}"

    PA = nc.alloc_psum_tensor("PA", [128, 2048], F32)
    PB = nc.alloc_psum_tensor("PB", [128, 2048], F32)
    PP = [PA[:, 0:1024], PA[:, 1024:2048]]
    AVp = PB[:, 0:512]
    M1 = [PB[:, 512:1024], PB[:, 1024:1536]]
    B7 = PB[:, 1536:2048]
    SEG = PA[:, 0:1024]
    TRP = PB[:, 0:1024].bitcast(BF16)
    TRP2 = PB[:, 1024:2048].bitcast(BF16)
    TRC = [PA[:, 0:1024].bitcast(BF16),
           PB[:, 0:1024].bitcast(BF16)]

    S = Sched(nc)

    def mm(ps, lhsT, rhs, start, stop, waits=(), inc=None):
        return S.emit("tensor",
                      lambda e, o=ps, l=lhsT, r=rhs: e.matmul(
                          o, l, r, start=start, stop=stop),
                      waits=waits, inc=inc)

    def mm_sc(ps, lhsT, rhs, start, stop, waits=(), inc=None):
        return S.emit("tensor", lambda e, o=ps, l=lhsT, r=rhs: e.matmul(
            o, l, r, start=start, stop=stop), waits=waits, inc=inc)

    def mm_avv(ps, lhsT, rhs, start, stop, waits=(), inc=None):
        return S.emit("tensor", lambda e, o=ps, l=lhsT, r=rhs: e.matmul(
            o, l, r, start=start, stop=stop), waits=waits, inc=inc)

    def mm_f1(ps, lhsT, rhs, start, stop, waits=(), inc=None):
        return S.emit("tensor", lambda e, o=ps, l=lhsT, r=rhs: e.matmul(
            o, l, r, start=start, stop=stop), waits=waits, inc=inc)

    def mm_f2(ps, lhsT, rhs, start, stop, waits=(), inc=None):
        return S.emit("tensor", lambda e, o=ps, l=lhsT, r=rhs: e.matmul(
            o, l, r, start=start, stop=stop), waits=waits, inc=inc)

    def mm_op(ps, lhsT, rhs, start, stop, waits=(), inc=None):
        return S.emit("tensor", lambda e, o=ps, l=lhsT, r=rhs: e.matmul(
            o, l, r, start=start, stop=stop), waits=waits, inc=inc)

    def tr(ps, in_, waits=(), inc=None):
        return S.emit("tensor",
                      lambda e, o=ps, i=in_: e.transpose(o, i, ident_sb[:, :]),
                      waits=waits, inc=inc)

    def tr_ln(ps, in_, waits=(), inc=None):
        return S.emit("tensor",
                      lambda e, o=ps, i=in_: e.transpose(o, i, ident_sb[:, :]),
                      waits=waits, inc=inc)

    def tr_qk(ps, in_, waits=(), inc=None):
        return S.emit("tensor",
                      lambda e, o=ps, i=in_: e.transpose(o, i, ident_sb[:, :]),
                      waits=waits, inc=inc)

    def dma(q, out_ap, in_ap, waits=(), sem="lds", serial=True):
        if serial:
            pre = S.counts.get(sem, 0)
            if pre > 0:
                waits = tuple(waits) + ((sem, pre),)
        return S.emit(q, lambda e, o=out_ap, i=in_ap: e.dma_start(o, i),
                      waits=waits, inc=sem, inc_by=16)

    def act(fn, waits=()):
        return S.emit("scalar", fn, waits=waits, inc="act")

    def dve(fn, waits=()):
        return S.emit("vector", fn, waits=waits, inc="dve")

    ev_wA = None
    for ec in range(16):
        ev_wA = dma("gpsimd", wA_sb[:, ec * 2304:(ec + 1) * 2304],
                    wqkvT_d[ec * 128:(ec + 1) * 128, :], sem="sgA",
                    serial=False)
    cast_pieces = []
    for j in range(16):
        cast_pieces.append((woT_bf[j * 128:(j + 1) * 128, :],
                            woT_d[j * 128:(j + 1) * 128, :], "sgW"))
    for j in range(32):
        k8 = NHT // 32
        cast_pieces.append((p1_bf[j * k8:(j + 1) * k8],
                            p1blk_d[j * k8:(j + 1) * k8], "sgP1"))
    for j in range(32):
        k8 = HID // 32
        cast_pieces.append((p2_bf[j * k8:(j + 1) * k8, :],
                            p2T_d[j * k8:(j + 1) * k8, :], "sgP2"))
    cast_ev = {"sgW": None, "sgP1": None, "sgP2": None}
    cast_i = [0]

    def emit_casts(n, waits=()):
        for _ in range(n):
            if cast_i[0] >= len(cast_pieces):
                return
            dst, srcp, sem = cast_pieces[cast_i[0]]
            cast_i[0] += 1
            cast_ev[sem] = dma("gpsimd", dst, srcp, waits=waits, sem=sem,
                               serial=False)

    ev_ini = None
    for dst, src in [(ident_sb[:, :], ident_d[:, :]),
                     (onescol_sb[:, :], onescol_d[:, :]),
                     (onesr_sb[:, :], onesr_d[:, :]),
                     (ones128_sb[:, :], ones128_d[:, :]),
                     (b1r_sb[:, :], b1r_d[:, :]),
                     (cos_sb[:, :], cosA_d[:, :]),
                     (sin_sb[:, :], sinA_d[:, :])]:
        ev_ini = dma("sync", dst, src, sem="sini", serial=False)
    ev_eps = dve(lambda e: e.memset(stats[:, 7:8], EPS))
    ev_invd = dve(lambda e: e.memset(stats[:, 15:16], 1.0 / D))

    ldx = {}
    ev_lnap = {}
    ev_red = {}
    ev_half = {}
    ev_P1ev = {}
    ev_qfl = {}
    ev_vc = {}
    ev_kc = {}
    ev_trqk = {}

    for tt in range(min(2, TT)):
        ldx[tt] = dma("sync", x_sb[tt % 2][:, :],
                      x_d[tt * 128:(tt + 1) * 128, :], sem=f"sx{tt % 2}")

    def ln_chain(xap, so, w_extra, out_ap, dump_ap):
        e_sq = act(lambda e: e.activation(dump_ap, xap, AF.Square,
                                          accum_out=stats[:, so + 6:so + 7]),
                   waits=w_extra)
        e_rs = dve(lambda e: e.reduce_sum(stats[:, so + 0:so + 1], xap,
                                          axis=AX.X), waits=w_extra)
        e_mean = dve(lambda e: e.tensor_scalar(
            stats[:, so + 1:so + 2], stats[:, so + 0:so + 1], 1.0 / D, None,
            ALU.mult), waits=_later(e_rs))
        e_msq = dve(lambda e: e.tensor_scalar(
            stats[:, so + 2:so + 3], stats[:, so + 1:so + 2],
            stats[:, so + 1:so + 2], None, ALU.mult), waits=_later(e_mean))
        e_var = dve(lambda e: e.scalar_tensor_tensor(
            stats[:, so + 3:so + 4], stats[:, so + 6:so + 7],
            stats[:, 15:16], stats[:, so + 2:so + 3], ALU.mult,
            ALU.subtract), waits=_later(e_msq, e_sq, ev_invd))
        e_ln = act(lambda e: e.activation(
            stats[:, so + 4:so + 5], stats[:, so + 3:so + 4], AF.Ln,
            bias=stats[:, 7:8]), waits=_later(e_var, ev_eps))
        e_rstd = act(lambda e: e.activation(
            stats[:, so + 4:so + 5], stats[:, so + 4:so + 5], AF.Exp,
            scale=-0.5), waits=_later(e_ln))
        e_nmrs = dve(lambda e: e.tensor_scalar(
            stats[:, so + 5:so + 6], stats[:, so + 1:so + 2],
            stats[:, so + 4:so + 5], -1.0, ALU.mult, ALU.mult),
            waits=_later(e_rstd))
        e_ap = act(lambda e: e.activation(
            out_ap, xap, AF.Identity, bias=stats[:, so + 5:so + 6],
            scale=stats[:, so + 4:so + 5]), waits=_later(e_nmrs))
        return e_ap, e_rs

    ev_eqx = {}
    pend_qktr = [None]
    pend_rope = [None]
    rope_o2 = {}

    def flush_qktr():
        if pend_qktr[0] is not None:
            pend_qktr[0]()
            pend_qktr[0] = None

    def flush_rope():
        if pend_rope[0] is not None:
            pend_rope[0]()
            pend_rope[0] = None

    for tt in range(min(4, TT)):
        ldx[tt] = dma("sync", xA[tt % 4][:, :],
                      x_d[tt * 128:(tt + 1) * 128, :], sem=f"sxa{tt % 4}")
    ev_lntr = {}

    def emit_chain(tt):
        xb = tt % 4
        so = (tt % 4) * 8
        wx = _later(ldx[tt], ev_lnap.get(tt - 4), ev_red.get(tt - 4))
        e_lnb, e_rs = ln_chain(xA[xb][:, :], so, wx, ln_bf[:, :],
                               atmpA[:, :])
        ev_lnap[tt] = e_lnb
        ev_red[tt] = e_rs
        if tt + 4 < TT:
            ldx[tt + 4] = dma("sync", xA[xb][:, :],
                              x_d[(tt + 4) * 128:(tt + 5) * 128, :],
                              waits=_later(e_lnb, e_rs), sem=f"sxa{xb}")
    emit_chain(0)
    for tt in range(TT):
        xb = tt % 4
        qkb = [qk_sb, qk2_sb][tt % 2]
        e_lnb = ev_lnap[tt]
        e_trg = [None, None]
        for g in range(2):
            wts = _later(e_lnb, ev_lntr.get(tt - 1),
                         ev_ini if tt == 0 else None)
            for j in range(8):
                e_trg[g] = tr_ln(
                    TRP[:, g * 1024 + j * 128:g * 1024 + (j + 1) * 128],
                    ln_bf[:, (8 * g + j) * 128:(8 * g + j + 1) * 128],
                    waits=wts, inc="pe" if j == 7 else None)
                wts = ()
        e_ev1 = dve(lambda e: e.tensor_copy(lnT_sb[:, 0:1024],
                                            TRP[:, 0:1024]),
                    waits=_later(e_trg[0]))
        e_ev2 = dve(lambda e: e.tensor_copy(lnT_sb[:, 1024:2048],
                                            TRP[:, 1024:2048]),
                    waits=_later(e_trg[1]))
        ev_lntr[tt] = e_ev2
        if tt + 1 < TT:
            emit_chain(tt + 1)
        flush_rope()
        segw = [512, 512, 512, 512, 256]
        ev_seg = [None] * 5
        for seg in range(5):
            pb = SEG[:, (seg % 2) * 512:(seg % 2) * 512 + segw[seg]]
            guard = {0: ev_half.get((tt - 1, 0)), 1: ev_half.get((tt - 1, 1)),
                     2: ev_half.get((tt, 0)), 3: ev_half.get((tt, 1)),
                     4: ev_half.get((tt, 0))}[seg]
            for ec in range(16):
                wts = _later(e_ev1, e_ev2, ev_wA, guard) if ec == 0 else ()
                ev_seg[seg] = mm(
                    pb, lnT_sb[:, ec * 128:(ec + 1) * 128],
                    wA_sb[:, ec * 2304 + seg * 512:
                          ec * 2304 + seg * 512 + segw[seg]],
                    ec == 0, ec == 15, waits=wts,
                    inc="pe" if ec == 15 else None)
            if seg == 1:
                flush_qktr()
            if seg < 4:
                e_cp = dve(lambda e, seg=seg, qkb=qkb: e.tensor_copy(
                    qkb[:, seg * 512:(seg + 1) * 512],
                    SEG[:, (seg % 2) * 512:(seg % 2) * 512 + 512]),
                    waits=_later(ev_seg[seg], ev_trqk.get(tt - 2)))
                ev_half[(tt, seg % 2)] = e_cp
            else:
                e_cpk = dve(lambda e, qkb=qkb: e.tensor_copy(
                    qkb[:, 2048:2176], SEG[:, 0:128]),
                    waits=_later(ev_seg[4], ev_trqk.get(tt - 2)))
                e_vc = dve(lambda e, tt=tt: e.tensor_copy(
                    V_sb[:, tt * 128:(tt + 1) * 128], SEG[:, 128:256]),
                    waits=_later(ev_seg[4]))
                ev_half[(tt, 0)] = e_vc
                ev_vc[tt] = e_vc
        def rope_emit(tt=tt, qkb=qkb, ev_h0=ev_half[(tt, 0)],
                      ev_h1=ev_half[(tt, 1)]):
            qk3 = qkb[:, :].rearrange("p (h d) -> p h d", d=128)
            A1 = qk3[:, :, 0:64]
            A2 = qk3[:, :, 64:128]
            cosb = cos_sb[:, tt * 64:(tt + 1) * 64].unsqueeze(1) \
                .broadcast_to((128, 17, 64))
            sinb = sin_sb[:, tt * 64:(tt + 1) * 64].unsqueeze(1) \
                .broadcast_to((128, 17, 64))
            rtv = [r[:, :].rearrange("p (h d) -> p h d", d=64) for r in rt]
            e_r1 = dve(lambda e, a=A1, b=cosb: e.tensor_tensor(
                rtv[0], a, b, ALU.mult),
                waits=_later(ev_h0, ev_h1, ev_ini))
            e_r2 = dve(lambda e, a=A2, b=sinb: e.tensor_tensor(
                rtv[1], a, b, ALU.mult))
            e_r3 = dve(lambda e, a=A1, b=sinb: e.tensor_tensor(
                rtv[2], a, b, ALU.mult))
            e_r4 = dve(lambda e, a=A2, b=cosb: e.tensor_tensor(
                rtv[3], a, b, ALU.mult))
            e_o1 = dve(lambda e, a=A1: e.tensor_tensor(
                a, rtv[0], rtv[1], ALU.subtract),
                waits=_later(e_r1, e_r2, e_r3, e_r4))
            e_o2 = dve(lambda e, a=A2: e.tensor_tensor(
                a, rtv[2], rtv[3], ALU.add), waits=_later(e_o1))

            rope_o2[tt] = e_o2

        def qktr(tt=tt, qkb=qkb):
            e_o2 = rope_o2[tt]
            gloc = tt % 4
            half = (tt // 4) % 2
            qgv = qstg[:, half * 8192:(half + 1) * 8192] \
                .rearrange("p (h c) -> p h c", c=512)
            prev = ev_eqx.get(tt - 1)
            gev = [None] * 3
            wts = _later(e_o2, prev[1] if prev else None,
                         prev[2] if prev else None)
            for j in range(8):
                gev[0] = tr_qk(TRP2[:, j * 128:(j + 1) * 128],
                            qkb[:, j * 128:(j + 1) * 128],
                            waits=wts, inc="pe" if j == 7 else None)
                wts = ()
            e_eq0 = dve(lambda e, gloc=gloc, qgv=qgv: e.tensor_copy(
                qgv[:, 0:8, gloc * 128:(gloc + 1) * 128],
                TRP2[:, 0:1024].rearrange("p (h c) -> p h c", c=128)),
                waits=_later(gev[0], ev_qfl.get(tt // 4 - 2)))
            wts = ()
            for j in range(8):
                gev[1] = tr_qk(TRP2[:, 1024 + j * 128:1024 + (j + 1) * 128],
                            qkb[:, (8 + j) * 128:(9 + j) * 128],
                            waits=wts, inc="pe" if j == 7 else None)
            e_eq1 = dve(lambda e, gloc=gloc, qgv=qgv: e.tensor_copy(
                qgv[:, 8:16, gloc * 128:(gloc + 1) * 128],
                TRP2[:, 1024:2048].rearrange("p (h c) -> p h c", c=128)),
                waits=_later(gev[1]))
            gev[2] = tr_qk(TRP2[:, 0:128], qkb[:, 16 * 128:17 * 128],
                        waits=_later(e_eq0), inc="pe")
            e_ek = dve(lambda e, tt=tt: e.tensor_copy(
                K_sb[:, tt * 128:(tt + 1) * 128], TRP2[:, 0:128]),
                waits=_later(gev[2]))
            ev_kc[tt] = e_ek
            ev_trqk[tt] = gev[2]
            ev_eqx[tt] = (e_eq0, e_eq1, e_ek)
            if tt % 4 == 3:
                g4 = tt // 4
                evs = _later(e_eq0, e_eq1, e_ek)
                for h in range(NH):
                    ev_qfl[g4] = dma(
                        "scalar", q_scr[h, :, g4 * 512:(g4 + 1) * 512],
                        qstg[:, half * 8192 + h * 512:
                             half * 8192 + (h + 1) * 512],
                        waits=evs if h == 0 else (), sem="sqf",
                        serial=False)
        pend_rope[0] = rope_emit
        pend_qktr[0] = qktr
        emit_casts(2, waits=_later(ldx[tt]))
    flush_rope()
    flush_qktr()
    emit_casts(1000)
    ev_wo = cast_ev["sgW"]
    ev_p1c = cast_ev["sgP1"]
    ev_p2c = cast_ev["sgP2"]

    PEA = ("pe", S.counts.get("pe", 0))
    DVEA = ("dve", S.counts.get("dve", 0))
    ACTA = ("act", S.counts.get("act", 0))
    SQFA = ("sqf", S.counts.get("sqf", 0))
    G0 = (PEA, DVEA, ACTA, SQFA)

    ev_exp = {}
    ev_sc2 = {}
    ring = {s: [G0] for s in range(8)}
    ev_qsld = {}
    qs_guard = {0: G0, 1: G0}
    ev_avevac = {}
    ev_dmm = {}
    ev_rc = {}
    ev_aT = {}
    ev_mm1 = {}
    ev_gelu = {}
    ev_cp1 = {}
    ev_p1ld = {}
    p1_guard = {0: G0, 1: G0}
    wr_guard = {0: G0, 1: G0}
    wr_i = [0]
    ev_mm2ep = {}
    ev_ysto = {}
    ev_opj = {}
    ev_mm2last = {}
    ev_x1w = {}
    ev_stgdone = {}
    ev_xld = {}
    x_guard = {0: G0, 1: G0}
    ln2slot_guard = {0: G0, 1: G0}
    trc_guard = {0: G0, 1: G0}
    att_tail = {}
    b2c_ev = [None]

    pend_gelu = [None]
    pend_tail = [None, None]

    def flush_gelu():
        if pend_gelu[0] is not None:
            pend_gelu[0]()
            pend_gelu[0] = None

    def flush_tail(i=None):
        for j in ((0, 1) if i is None else (i,)):
            if pend_tail[j] is not None:
                pend_tail[j]()
                pend_tail[j] = None

    def emit_av(c, h, p, s_adds):
        h_g = c * NH + h
        g = h_g * NP + p
        slot = g % 8
        evav = None
        for kk in (2 * p, 2 * p + 1):
            w = ()
            if kk == 2 * p:
                w = _later(ev_exp[g],
                           ev_avevac.get(h_g - 1) if p == 0 else None,
                           G0 if h_g == 0 and p == 0 else None)
            evav = mm_avv(AVp, V_sb[:, kk * 128:(kk + 1) * 128],
                      E_sb[:, slot * 1024 + (kk - 2 * p) * 512:
                           slot * 1024 + (kk - 2 * p + 1) * 512],
                      p == 0 and kk == 2 * p,
                      p == NP - 1 and kk == 2 * p + 1,
                      waits=w, inc="pe")
        ring[slot] = [evav, s_adds[p]]
        return evav

    def emit_head(c, h, take_fill):
        h_g = c * NH + h
        qpar = h % 2
        Sb = SS[h % 2]
        if (c, h) not in ev_qsld:
            ev_qsld[(c, h)] = dma(
                "sync", Qs[qpar][:, :], q_scr[h, :, c * 512:(c + 1) * 512],
                waits=_later(qs_guard[qpar],
                             G0 if c == 0 and h <= 1 else None),
                sem=f"sq{qpar}")
        nx_c, nx_h = (c, h + 1) if h + 1 < NH else (c + 1, 0)
        if nx_c < NCH and (nx_c, nx_h) not in ev_qsld:
            ev_qsld[(nx_c, nx_h)] = dma(
                "sync", Qs[nx_h % 2][:, :],
                q_scr[nx_h, :, nx_c * 512:(nx_c + 1) * 512],
                waits=_later(qs_guard[nx_h % 2],
                             G0 if c == 0 and h == 0 else None),
                sem=f"sq{nx_h % 2}")
        flush_tail(0)
        s_adds = {}
        evav = [None]
        for p in range(NP):
            g = h_g * NP + p
            slot = g % 8
            pp = PP[g % 2]
            w0 = _later(ev_qsld[(c, h)], ev_exp.get(g - 2),
                        ev_stgdone.get(c - 1) if (h == 0 and p == 0)
                        else None,
                        G0 if (c == 0 and h == 0 and p == 0) else None)
            mm_sc(pp[:, 0:512], K_sb[:, (2 * p) * 128:(2 * p + 1) * 128],
                  Qs[qpar][:, :], True, True, waits=w0)
            ev_sc2[g] = mm_sc(pp[:, 512:1024],
                           K_sb[:, (2 * p + 1) * 128:(2 * p + 2) * 128],
                           Qs[qpar][:, :], True, True, inc="pe")
            if p == 2:
                flush_tail(1)
            if p >= 2:
                evav[0] = emit_av(c, h, p - 2, s_adds)
                take_fill(4)
            ev_exp[g] = act(
                lambda e, pp=pp, slot=slot: e.activation(
                    E_sb[:, slot * 1024:(slot + 1) * 1024], pp[:, :],
                    AF.Exp, scale=ISQ),
                waits=_later(ev_sc2[g], *ring[slot]))
            if p == 0:
                s_adds[p] = dve(lambda e, slot=slot, Sb=Sb: e.tensor_tensor(
                    Sb[:, :], E_sb[:, slot * 1024:slot * 1024 + 512],
                    E_sb[:, slot * 1024 + 512:(slot + 1) * 1024], ALU.add),
                    waits=_later(ev_exp[g], ev_dmm.get(h_g - 2)))
            else:
                dve(lambda e, slot=slot: e.tensor_tensor(
                    tmpP[:, :], E_sb[:, slot * 1024:slot * 1024 + 512],
                    E_sb[:, slot * 1024 + 512:(slot + 1) * 1024], ALU.add),
                    waits=_later(ev_exp[g]))
                s_adds[p] = dve(lambda e, Sb=Sb: e.tensor_tensor(
                    Sb[:, :], Sb[:, :], tmpP[:, :], ALU.add))
        evav[0] = emit_av(c, h, NP - 2, s_adds)
        take_fill(4)
        evav[0] = emit_av(c, h, NP - 1, s_adds)
        take_fill(4)
        qs_guard[qpar] = ev_sc2[h_g * NP + NP - 1]

        def tail0(h_g=h_g, evav=evav[0]):
            ev_avevac[h_g] = dve(lambda e: e.tensor_copy(
                avs[:, :], AVp[:, :]), waits=_later(evav))

        def tail1(h_g=h_g, h=h, c=c, Sb=Sb, s_last=s_adds[NP - 1]):
            ev_dmm[h_g] = mm(B7[:, :], ones128_sb[:, :], Sb[:, :],
                             True, True,
                             waits=_later(s_last, ev_rc.get(h_g - 1),
                                          b2c_ev[0] if h == 0 else None,
                                          G0 if h_g == 0 else None),
                             inc="pe")
            ev_rc[h_g] = dve(lambda e: e.reciprocal(rD_sb[:, :], B7[:, :]),
                             waits=_later(ev_dmm[h_g]))
            ev_aT[h_g] = dve(lambda e, h=h: e.tensor_tensor(
                aT_sb[:, h * 512:(h + 1) * 512], avs[:, :], rD_sb[:, :],
                ALU.mult),
                waits=_later(ev_opj.get(c - 1) if h == 0 else None,
                             G0 if h_g == 0 else None))
        pend_tail[0] = tail0
        pend_tail[1] = tail1

    def filler_stream(cf):
        for ht in range(NHT):
            pb = ht % 2
            for ec in range(16):
                def emit_one(ht=ht, ec=ec, pb=pb):
                    if ec == 0:
                        flush_gelu()
                        for hh in (ht, ht + 1):
                            if hh < NHT and (cf, hh) not in ev_p1ld:
                                ev_p1ld[(cf, hh)] = dma(
                                    ("sync", "gpsimd")[hh % 2],
                                    p1s[hh % 2][:, :], p1_bf[hh],
                                    waits=_later(p1_guard[hh % 2],
                                                 ev_p1c if cf == 0 and
                                                 hh <= 1 else None),
                                    sem=f"sp1{hh % 2}")
                        w0 = _later(ev_p1ld[(cf, ht)],
                                    ev_cp1.get((cf, ht - 2)),
                                    ev_stgdone.get(cf),
                                    G0 if cf == 0 and ht <= 1 else None)
                    else:
                        w0 = ()
                    ev = mm_f1(M1[pb], p1s[pb][:, ec * 128:(ec + 1) * 128],
                            stg[:, ec * 512:(ec + 1) * 512],
                            ec == 0, ec == 15, waits=w0,
                            inc="pe" if ec == 15 else None)
                    if ec == 15:
                        ev_mm1[(cf, ht)] = ev
                        p1_guard[pb] = ev
                        ev_cp1[(cf, ht)] = dve(
                            lambda e, ht=ht, pb=pb: e.tensor_copy(
                                hT_sb[:, ht * 512:(ht + 1) * 512], M1[pb]),
                            waits=_later(ev, G0 if cf == 0 else None))
                yield emit_one

    def emit_mm2(cf):
        PSs = [PA, PB]
        tail = att_tail.get(cf + 1, att_tail[cf])
        stop_ev = {}
        for ht in range(NHT):
            ev_gelu[(cf, ht)] = act(
                lambda e, ht=ht: e.activation(
                    hT_sb[:, ht * 512:(ht + 1) * 512],
                    hT_sb[:, ht * 512:(ht + 1) * 512], AF.Gelu,
                    bias=b1r_sb[:, ht:ht + 1]),
                waits=_later(ev_cp1[(cf, ht)], ev_ini))
        W6 = [wr[0], wr[1]] + [stg[:, i * 2048:(i + 1) * 2048]
                               for i in range(4)]
        NW = 6
        ldev = {}
        grp_last = {}

        def issue_load(j):
            hcj = j % NHT
            sl = j % NW
            w = _later(grp_last.get(j - NW),
                       wr_guard[sl] if (j < NW and sl < 2) else None,
                       ev_mm1[(cf, NHT - 1)] if (j < NW and sl >= 2)
                       else None,
                       ev_p2c if cf == 0 and j < NW else None)
            ldev[j] = dma(("sync", "gpsimd")[sl % 2], W6[sl][:, :],
                          p2_bf[hcj * 128:(hcj + 1) * 128, :], waits=w,
                          sem=f"swr{sl % 2}")
        for j in range(NW):
            issue_load(j)
        for pr in range(2):
            tls = (2, 3) if pr == 0 else (0, 1)
            if pr == 1:
                for tl in (0, 1):
                    ev_x1r = dma(
                        "gpsimd", x1t[tl % 2][:, :],
                        x1_scr[(cf * 4 + tl) * 128:(cf * 4 + tl + 1) * 128,
                               :],
                        waits=_later(ev_x1w[(cf, tl)],
                                     ev_mm2ep[(cf, tl + 2)]),
                        sem=f"sx1r{tl % 2}")
                    ev_x1w[(cf, tl, "r")] = ev_x1r
            for hc in range(NHT):
                j = pr * NHT + hc
                sl = j % NW
                ev_w = ldev[j]
                last = None
                for ti, tl in enumerate(tls):
                    ps = PSs[ti]
                    for s4 in range(4):
                        w = ()
                        if hc == 0 and s4 == 0:
                            if pr == 0:
                                w = _later(ev_w, ev_gelu[(cf, hc)],
                                           tail[0], tail[1],
                                           ev_stgdone[cf],
                                           ev_cp1[(cf, NHT - 1)]
                                           if ti == 1 else None)
                            else:
                                w = _later(ev_w, ev_mm2ep[(cf, tls[ti] + 2)])
                        elif s4 == 0:
                            w = _later(ev_w, ev_gelu[(cf, hc)])
                        last = mm_f2(ps[:, s4 * 512:(s4 + 1) * 512],
                                     hT_sb[:, hc * 512 + tl * 128:
                                     hc * 512 + (tl + 1) * 128],
                                     W6[sl][:, s4 * 512:(s4 + 1) * 512],
                                     hc == 0, hc == NHT - 1, waits=w,
                                     inc="pe" if (s4 == 3 and
                                                  (ti == 1 or
                                                   hc == NHT - 1))
                                     else None)
                        if s4 == 3 and hc == NHT - 1:
                            stop_ev[tl] = last
                grp_last[j] = last
                if j + NW < 2 * NHT:
                    issue_load(j + NW)
            for ti, tl in enumerate(tls):
                yv = y_stage[:, ti * 2048:(ti + 1) * 2048]
                wts = _later(stop_ev[tl],
                             ev_x1w.get((cf, tl, "r")),
                             ev_ysto.get((cf, tl + 2)),
                             ev_ysto.get((cf - 1, tl - 2)) if pr == 0
                             else None,
                             tail[0] if pr == 0 else None)
                ev_mm2ep[(cf, tl)] = dve(
                    lambda e, yv=yv, ps=PSs[ti], tl=tl: e.tensor_tensor(
                        yv, ps[:, :], x1t[tl % 2][:, :], ALU.add),
                    waits=wts)
                ev_ysto[(cf, tl)] = dma(
                    "scalar",
                    y_d[(cf * 4 + tl) * 128:(cf * 4 + tl + 1) * 128, :],
                    yv, waits=_later(ev_mm2ep[(cf, tl)]), sem="ssy",
                    serial=False)
        wr_guard[0] = grp_last[126]
        wr_guard[1] = grp_last[127]
        ev_mm2last[cf] = stop_ev[1]
        for s in range(8):
            ring[s] = ring[s] + [ev_ysto[(cf, 2 if s < 4 else 3)],
                                 ev_ysto[(cf, 0 if s < 4 else 1)]]

    def emit_outproj(c):
        PSs = [PA, PB]
        if b2c_ev[0] is None:
            e_l = dma("sync", b2b_sb[0:1, :], b2row_d[:, :],
                      waits=_later(G0), sem="sini", serial=False)
            e_c = None
            for s4 in range(4):
                e_m = mm(B7[:, :], onesr_sb[0:1, :],
                         b2b_sb[0:1, s4 * 512:(s4 + 1) * 512], True, True,
                         waits=_later(e_l, ev_aT[c * NH + NH - 1],
                                      e_c), inc="pe")
                e_c = dve(lambda e, s4=s4: e.tensor_copy(
                    b2b_sb[:, s4 * 512:(s4 + 1) * 512], B7[:, :]),
                    waits=_later(e_m))
            b2c_ev[0] = e_c
        for tl in (0, 1):
            ev_xld[(c, tl)] = dma(
                "gpsimd", x_sb[tl % 2][:, :],
                x_d[(c * 4 + tl) * 128:(c * 4 + tl + 1) * 128, :],
                waits=_later(x_guard[tl % 2], G0 if c == 0 else None),
                sem=f"sx{tl % 2}")
        stop_ev = {}
        ev_x1add = {}
        ev_l2ap = {}
        ev_b2ad = {}
        e_evc = {}
        stgv = stg[:, :].rearrange("p (e c) -> p e c", c=512)

        W4 = [wr[0], wr[1], p1s[0], p1s[1]]
        W4sem = ["swr0", "swr1", "sp10", "sp11"]
        ldev = {}
        grp_last = {}

        def issue_load(j):
            dhj = j % 16
            sl = j % 4
            w = _later(grp_last.get(j - 4),
                       (wr_guard[sl] if sl < 2 else
                        ev_mm1.get((c - 1, NHT - 1))) if j < 4 else None,
                       G0 if c == 0 and j < 4 else None,
                       ev_wo if c == 0 and j < 4 else None)
            ldev[j] = dma(("sync", "gpsimd")[sl % 2], W4[sl][:, :],
                          woT_bf[dhj * 128:(dhj + 1) * 128, :], waits=w,
                          sem=W4sem[sl])
        for j in range(4):
            issue_load(j)

        def emit_pr(pr):
            tls = (pr * 2, pr * 2 + 1)
            for dh in range(16):
                j = pr * 16 + dh
                sl = j % 4
                ev_w = ldev[j]
                last = None
                for ti, tl in enumerate(tls):
                    ps = PSs[ti]
                    for s4 in range(4):
                        w = ()
                        if dh == 0 and s4 == 0:
                            if pr == 0:
                                w = _later(ev_w, ev_aT[c * NH + NH - 1],
                                           att_tail[c][0],
                                           ev_mm2ep.get((c - 1, 1)),
                                           b2c_ev[0])
                            else:
                                w = _later(ev_w, ev_x1add[tls[ti] - 2])
                        elif s4 == 0:
                            w = _later(ev_w)
                        last = mm_op(ps[:, s4 * 512:(s4 + 1) * 512],
                                     aT_sb[:, dh * 512 + tl * 128:
                                     dh * 512 + (tl + 1) * 128],
                                     W4[sl][:, s4 * 512:(s4 + 1) * 512],
                                     dh == 0, dh == 15, waits=w,
                                     inc="pe" if (s4 == 3 and
                                                  (ti == 1 or dh == 15))
                                     else None)
                        if s4 == 3 and dh == 15:
                            stop_ev[tl] = last
                grp_last[j] = last
                if j + 4 < 32:
                    issue_load(j + 4)
            ev_opj[c] = stop_ev[tls[1]]

        def x1_part(tl):
            xb = tl % 2
            ev_x1add[tl] = dve(
                lambda e, xb=xb, ps=PSs[tl % 2]: e.tensor_tensor(
                    x1t[xb][:, :], ps[:, :], x_sb[xb][:, :], ALU.add),
                waits=_later(stop_ev[tl], ev_xld[(c, tl)],
                             ev_x1w.get((c, tl - 2)),
                             ev_b2ad.get(tl - 2),
                             G0 if c == 0 else None))
            x_guard[xb] = ev_x1add[tl]
            if tl + 2 <= 3:
                ev_xld[(c, tl + 2)] = dma(
                    "gpsimd", x_sb[xb][:, :],
                    x_d[(c * 4 + tl + 2) * 128:(c * 4 + tl + 3) * 128, :],
                    waits=_later(ev_x1add[tl]), sem=f"sx{xb}")

        def ln2_part(tl):
            xb = tl % 2
            so = xb * 8
            wx = _later(ev_x1add[tl], ln2slot_guard[xb],
                        G0 if c == 0 else None)
            e_l2, _ = ln_chain(
                x1t[xb][:, :], so, wx,
                ln2b4[:, xb * 2048:(xb + 1) * 2048],
                ln2b4[:, xb * 2048:(xb + 1) * 2048])
            ev_l2ap[tl] = e_l2
            ev_b2ad[tl] = dve(
                lambda e, xb=xb: e.tensor_tensor(
                    x1t[xb][:, :], x1t[xb][:, :], b2b_sb[:, :], ALU.add),
                waits=_later(e_l2, b2c_ev[0]))
            if tl <= 1:
                ev_x1w[(c, tl)] = dma(
                    "gpsimd",
                    x1_scr[(c * 4 + tl) * 128:(c * 4 + tl + 1) * 128, :],
                    x1t[xb][:, :], waits=_later(ev_b2ad[tl]),
                    sem="sx1w", serial=False)

        def tr_part(tl):
            xb = tl % 2
            buf = TRC[xb]
            wts = _later(ev_l2ap[tl], ev_x1add[2 + xb], trc_guard[xb])
            e_t = None
            for j in range(16):
                e_t = tr(buf[:, j * 128:(j + 1) * 128],
                         ln2b4[:, xb * 2048 + j * 128:
                               xb * 2048 + (j + 1) * 128],
                         waits=wts, inc="pe" if j == 15 else None)
                wts = ()
            ln2slot_guard[xb] = e_t
            e_evc[tl] = dve(lambda e, tl=tl, buf=buf: e.tensor_copy(
                stgv[:, :, tl * 128:(tl + 1) * 128],
                buf[:, :].rearrange("p (e c) -> p e c", c=128)),
                waits=_later(e_t, ev_mm1.get((c - 1, NHT - 1)),
                             ev_mm2last.get(c - 1)))
            trc_guard[xb] = e_evc[tl]

        emit_pr(0)
        x1_part(0)
        x1_part(1)
        ln2_part(0)
        ln2_part(1)
        emit_pr(1)
        x1_part(2)
        x1_part(3)
        tr_part(0)
        tr_part(1)
        ln2_part(2)
        ln2_part(3)
        tr_part(2)
        tr_part(3)
        ev_stgdone[c] = e_evc[3]
        wr_guard[0] = grp_last[28]
        wr_guard[1] = grp_last[29]
        p1_guard[0] = grp_last[30]
        p1_guard[1] = grp_last[31]

    for k in range(NCH + 1):
        cA, cF = k, k - 1
        fs = filler_stream(cF) if cF >= 0 else iter(())

        def take_fill(n, fs=fs):
            for _ in range(n):
                try:
                    next(fs)()
                except StopIteration:
                    break
        if cA < NCH:
            for h in range(NH):
                emit_head(cA, h, take_fill)
            flush_tail()
            att_tail[cA] = (ev_exp[(cA * NH + NH - 1) * NP + NP - 1],
                            ev_aT[cA * NH + NH - 1])
        for f in fs:
            f()
        flush_gelu()
        if cF >= 0:
            emit_mm2(cF)
        if cA < NCH:
            emit_outproj(cA)

    dma_sems = [k for k in S.counts if k not in ("pe", "act", "dve")]
    finals = tuple((k, S.counts[k]) for k in dma_sems)
    S.emit("scalar", lambda e: e.nop(),
           waits=tuple((k, v) for k, v in finals
                       if k in ("ssy", "sx1w", "sqf")))
    S.emit("sync", lambda e: e.nop(), waits=finals)

    with nc.Block() as block:
        S.replay(block)
    nc.finalize()
    return nc


def _prep(w_attn, w_out, proj1, proj1_bias, proj2, proj2_bias, T):
    TT = T // 128
    d = {}
    d["wqkvT"] = np.ascontiguousarray(w_attn.T)
    d["woT"] = np.ascontiguousarray(w_out.T)
    d["p1blk"] = np.ascontiguousarray(
        proj1.reshape(NHT, 128, 16, 128).transpose(0, 3, 2, 1)
        .reshape(NHT, 128, 2048))
    d["p2T"] = np.ascontiguousarray(proj2.T)
    d["b1r"] = np.ascontiguousarray(proj1_bias.reshape(NHT, 128).T)
    d["b2row"] = np.ascontiguousarray(proj2_bias.reshape(1, D))
    j = np.arange(0, HD, 2, dtype=np.float64)
    pos = (10000.0 ** ((-2.0 * j - 1.0) / HD)).astype(np.float32)
    ang = np.arange(T, dtype=np.float32)[:, None] * pos[None, :]
    cos = np.cos(ang).astype(np.float32)
    sin = np.sin(ang).astype(np.float32)
    d["cosA"] = np.ascontiguousarray(
        cos.reshape(TT, 128, 64).transpose(1, 0, 2).reshape(128, TT * 64))
    d["sinA"] = np.ascontiguousarray(
        sin.reshape(TT, 128, 64).transpose(1, 0, 2).reshape(128, TT * 64))
    d["ident"] = np.eye(128, dtype=ml_dtypes.bfloat16)
    d["onescol"] = np.ones((128, 1), dtype=np.float32)
    d["ones128"] = np.ones((128, 128), dtype=np.float32)
    d["onesr"] = np.ones((1, 128), dtype=np.float32)
    return d


def kernel(L0_attn_x, L0_attn_w_attn, L0_attn_w_out, L0_ffn_proj1,
           L0_ffn_proj1_bias, L0_ffn_proj2, L0_ffn_proj2_bias,
           L0_ffn_gate_w=None, **_unused):
    x = np.asarray(L0_attn_x, dtype=np.float32)
    B, T, _ = x.shape
    shared = _prep(np.asarray(L0_attn_w_attn, np.float32),
                   np.asarray(L0_attn_w_out, np.float32),
                   np.asarray(L0_ffn_proj1, np.float32),
                   np.asarray(L0_ffn_proj1_bias, np.float32),
                   np.asarray(L0_ffn_proj2, np.float32),
                   np.asarray(L0_ffn_proj2_bias, np.float32), T)
    nc = build_program(T)
    in_maps = []
    for b in range(B):
        m = dict(shared)
        m["x"] = np.ascontiguousarray(x[b])
        in_maps.append(m)
    res = run_bass_kernel_spmd(nc, in_maps, core_ids=list(range(B)))
    out = np.stack([np.asarray(r["y"], dtype=np.float32)
                    for r in res.results], axis=0)
    return out


# version 20
# speedup vs baseline: 1.1680x; 1.0477x over previous
import numpy as np
import ml_dtypes
import concourse.bass as bass
import concourse.mybir as mybir
from concourse.bass_utils import run_bass_kernel_spmd

F32 = mybir.dt.float32
BF16 = mybir.dt.bfloat16
AF = mybir.ActivationFunctionType
ALU = mybir.AluOpType
AX = mybir.AxisListType

D = 2048
NH = 16
HD = 128
HID = 8192
NHT = HID // 128
EPS = 1e-5
ISQ = float(1.0 / np.sqrt(128.0))

ENGINES = ("tensor", "scalar", "vector", "sync", "gpsimd")


class Sched:

    def __init__(self, nc):
        self.nc = nc
        self.items = {e: [] for e in ENGINES}
        self.sems = {}
        self.counts = {}

    def _sem(self, name):
        if name not in self.sems:
            self.sems[name] = self.nc.alloc_semaphore(name=name)
            self.counts[name] = 0
        return self.sems[name]

    def emit(self, eng, fn, waits=(), inc=None, inc_by=1):
        ev = None
        if inc is not None:
            self._sem(inc)
            self.counts[inc] += inc_by
            ev = (inc, self.counts[inc])
        ws = tuple((s, v) for (s, v) in waits if s is not None and v > 0)
        for s, _ in ws:
            self._sem(s)
        self.items[eng].append((fn, ws, inc, inc_by))
        return ev

    def replay(self, block):
        for ename in ENGINES:
            seq = self.items[ename]
            if not seq:
                continue

            def mk(seq):
                def body(eng):
                    wm = {}
                    for fn, waits, inc, inc_by in seq:
                        for (sname, val) in waits:
                            if wm.get(sname, 0) >= val:
                                continue
                            eng.wait_ge(self.sems[sname], val)
                            wm[sname] = val
                        ins = fn(eng)
                        if inc is not None:
                            ins.then_inc(self.sems[inc], inc_by)
                return body
            getattr(block, ename)(mk(seq))


def _later(*evs):
    best = {}

    def add(ev):
        if ev is None:
            return
        if isinstance(ev, tuple) and len(ev) == 2 and isinstance(ev[0], str):
            s, v = ev
            if best.get(s, -1) < v:
                best[s] = v
        else:
            for e in ev:
                add(e)
    for ev in evs:
        add(ev)
    return tuple(best.items())


def build_program(T):
    TT = T // 128
    KT = T // 128
    NP = KT // 2
    NCH = T // 512
    assert TT % 4 == 0

    nc = bass.Bass("TRN2", target_bir_lowering=False)

    x_d = nc.dram_tensor("x", [T, D], F32, kind="ExternalInput")
    wqkvT_d = nc.dram_tensor("wqkvT", [D, 2304], F32, kind="ExternalInput")
    woT_d = nc.dram_tensor("woT", [D, D], F32, kind="ExternalInput")
    p1blk_d = nc.dram_tensor("p1blk", [NHT, 128, 2048], F32,
                             kind="ExternalInput")
    p2T_d = nc.dram_tensor("p2T", [HID, D], F32, kind="ExternalInput")
    b1r_d = nc.dram_tensor("b1r", [128, NHT], F32, kind="ExternalInput")
    b2row_d = nc.dram_tensor("b2row", [1, D], F32, kind="ExternalInput")
    cosA_d = nc.dram_tensor("cosA", [128, TT * 64], F32, kind="ExternalInput")
    sinA_d = nc.dram_tensor("sinA", [128, TT * 64], F32, kind="ExternalInput")
    ident_d = nc.dram_tensor("ident", [128, 128], BF16, kind="ExternalInput")
    onescol_d = nc.dram_tensor("onescol", [128, 1], F32, kind="ExternalInput")
    onesr_d = nc.dram_tensor("onesr", [1, 128], F32, kind="ExternalInput")
    ones128_d = nc.dram_tensor("ones128", [128, 128], F32,
                               kind="ExternalInput")
    y_d = nc.dram_tensor("y", [T, D], F32, kind="ExternalOutput")
    woT_bf = nc.dram_tensor("woT_bf", [D, D], BF16, kind="Internal")
    p1_bf = nc.dram_tensor("p1_bf", [NHT, 128, 2048], BF16, kind="Internal")
    p2_bf = nc.dram_tensor("p2_bf", [HID, D], BF16, kind="Internal")
    q_scr = nc.dram_tensor("q_scr", [NH, 128, T], BF16, kind="Internal")
    x1_scr = nc.dram_tensor("x1_scr", [T, D], F32, kind="Internal")

    ORIGIN = 20480
    LIMIT = 229376 - 128
    off = [ORIGIN]

    def sb(name, cols, dt, at=None, parts=128):
        nbytes = cols * (2 if dt == BF16 else 4)
        if at is None:
            at = off[0]
            off[0] = at + ((nbytes + 31) // 32) * 32
        return nc.alloc_sbuf_tensor_at(name, [parts, cols], dt, offset=at)

    K_sb = sb("K_sb", T, BF16)
    V_sb = sb("V_sb", T, BF16)
    x_sb = [sb("x0", D, F32), sb("x1b", D, F32)]
    stats = sb("stats", 32, F32)
    ident_sb = sb("ident_s", 128, BF16)
    onescol_sb = sb("onescol_s", 1, F32)
    b1r_sb = sb("b1r_s", NHT, F32)
    onesr_sb = sb("onesr_s", 128, F32, parts=1)
    ones128_sb = sb("ones128_s", 128, F32)

    AR0 = off[0]
    E_sb = sb("E_sb", 8 * 1024, BF16, at=AR0)
    y_stage = sb("y_stage", 2 * 2048, F32, at=AR0)
    aT_sb = sb("aT_sb", NH * 512, BF16, at=AR0 + 16384)
    Qs = [sb("Qs0", 512, BF16, at=AR0 + 32768),
          sb("Qs1", 512, BF16, at=AR0 + 33792)]
    hT_sb = sb("hT", NHT * 512, BF16, at=AR0 + 34816)
    stg = sb("stg", 16 * 512, BF16, at=AR0 + 100352)
    ln2b4 = sb("ln2b4", 2 * 2048, BF16, at=AR0 + 116736)
    avs = sb("avs", 512, F32, at=AR0 + 116736)
    p1s = [sb("p1a", 2048, BF16, at=AR0 + 124928),
           sb("p1b", 2048, BF16, at=AR0 + 129024)]
    wr = [sb("wr0", 2048, BF16, at=AR0 + 133120),
          sb("wr1", 2048, BF16, at=AR0 + 137216)]
    x1t = [sb("x1t0", 2048, F32, at=AR0 + 141312),
           sb("x1t1", 2048, F32, at=AR0 + 149504)]
    b2b_sb = sb("b2b", 2048, F32, at=AR0 + 157696)
    SS = [sb("S_sb", 512, F32, at=AR0 + 165888),
          sb("S2_sb", 512, F32, at=AR0 + 167936)]
    tmpP = sb("tmpP", 512, F32, at=AR0 + 169984)
    rD_sb = sb("rD_sb", 512, F32, at=AR0 + 172032)
    CH_END = AR0 + 174080
    wA_sb = sb("wA", 16 * 2304, BF16, at=AR0)
    cos_sb = sb("cosA_s", TT * 64, F32, at=AR0 + 73728)
    sin_sb = sb("sinA_s", TT * 64, F32, at=AR0 + 81920)
    rt = [sb(f"rt{i}", 1088, BF16, at=AR0 + 90112 + i * 2176)
          for i in range(4)]
    qk_sb = sb("qk", 2176, BF16, at=AR0 + 98816)
    lnT_sb = sb("lnT", D, BF16, at=AR0 + 103168)
    ln_bf = sb("ln_bf", D, BF16, at=AR0 + 107264)
    qstg = sb("qstg", 2 * 8192, BF16, at=AR0 + 111360)
    atmpA = sb("atmpA", D, BF16, at=AR0 + 144128)
    xA = [x_sb[0], x_sb[1],
          sb("xA2", D, F32, at=AR0 + 148224),
          sb("xA3", D, F32, at=AR0 + 156416)]
    qk2_sb = sb("qk2", 2176, BF16, at=AR0 + 164608)
    A_END = AR0 + 168960
    assert max(CH_END, A_END) <= LIMIT, f"sbuf overflow {CH_END} > {LIMIT}"

    PA = nc.alloc_psum_tensor("PA", [128, 2048], F32)
    PB = nc.alloc_psum_tensor("PB", [128, 2048], F32)
    PP = [PA[:, 0:1024], PA[:, 1024:2048]]
    AVp = PB[:, 0:512]
    M1 = [PB[:, 512:1024], PB[:, 1024:1536]]
    B7 = PB[:, 1536:2048]
    SEG = PA[:, 0:1024]
    TRP = PB[:, 0:1024].bitcast(BF16)
    TRP2 = PB[:, 1024:2048].bitcast(BF16)
    TRC = [PA[:, 0:1024].bitcast(BF16),
           PB[:, 0:1024].bitcast(BF16)]

    S = Sched(nc)

    def mm(ps, lhsT, rhs, start, stop, waits=(), inc=None):
        return S.emit("tensor",
                      lambda e, o=ps, l=lhsT, r=rhs: e.matmul(
                          o, l, r, start=start, stop=stop),
                      waits=waits, inc=inc)

    def mm_sc(ps, lhsT, rhs, start, stop, waits=(), inc=None):
        return S.emit("tensor", lambda e, o=ps, l=lhsT, r=rhs: e.matmul(
            o, l, r, start=start, stop=stop), waits=waits, inc=inc)

    def mm_avv(ps, lhsT, rhs, start, stop, waits=(), inc=None):
        return S.emit("tensor", lambda e, o=ps, l=lhsT, r=rhs: e.matmul(
            o, l, r, start=start, stop=stop), waits=waits, inc=inc)

    def mm_f1(ps, lhsT, rhs, start, stop, waits=(), inc=None):
        return S.emit("tensor", lambda e, o=ps, l=lhsT, r=rhs: e.matmul(
            o, l, r, start=start, stop=stop), waits=waits, inc=inc)

    def mm_f2(ps, lhsT, rhs, start, stop, waits=(), inc=None):
        return S.emit("tensor", lambda e, o=ps, l=lhsT, r=rhs: e.matmul(
            o, l, r, start=start, stop=stop), waits=waits, inc=inc)

    def mm_op(ps, lhsT, rhs, start, stop, waits=(), inc=None):
        return S.emit("tensor", lambda e, o=ps, l=lhsT, r=rhs: e.matmul(
            o, l, r, start=start, stop=stop), waits=waits, inc=inc)

    def tr(ps, in_, waits=(), inc=None):
        return S.emit("tensor",
                      lambda e, o=ps, i=in_: e.transpose(o, i, ident_sb[:, :]),
                      waits=waits, inc=inc)

    def tr_ln(ps, in_, waits=(), inc=None):
        return S.emit("tensor",
                      lambda e, o=ps, i=in_: e.transpose(o, i, ident_sb[:, :]),
                      waits=waits, inc=inc)

    def tr_qk(ps, in_, waits=(), inc=None):
        return S.emit("tensor",
                      lambda e, o=ps, i=in_: e.transpose(o, i, ident_sb[:, :]),
                      waits=waits, inc=inc)

    def dma(q, out_ap, in_ap, waits=(), sem="lds", serial=True):
        if serial:
            pre = S.counts.get(sem, 0)
            if pre > 0:
                waits = tuple(waits) + ((sem, pre),)
        return S.emit(q, lambda e, o=out_ap, i=in_ap: e.dma_start(o, i),
                      waits=waits, inc=sem, inc_by=16)

    def act(fn, waits=()):
        return S.emit("scalar", fn, waits=waits, inc="act")

    def dve(fn, waits=()):
        return S.emit("vector", fn, waits=waits, inc="dve")

    ev_wA = None
    for ec in range(16):
        ev_wA = dma("gpsimd", wA_sb[:, ec * 2304:(ec + 1) * 2304],
                    wqkvT_d[ec * 128:(ec + 1) * 128, :], sem="sgA",
                    serial=False)
    cast_pieces = []
    for j in range(16):
        cast_pieces.append((woT_bf[j * 128:(j + 1) * 128, :],
                            woT_d[j * 128:(j + 1) * 128, :], "sgW"))
    for j in range(32):
        k8 = NHT // 32
        cast_pieces.append((p1_bf[j * k8:(j + 1) * k8],
                            p1blk_d[j * k8:(j + 1) * k8], "sgP1"))
    for j in range(32):
        k8 = HID // 32
        cast_pieces.append((p2_bf[j * k8:(j + 1) * k8, :],
                            p2T_d[j * k8:(j + 1) * k8, :], "sgP2"))
    cast_ev = {"sgW": None, "sgP1": None, "sgP2": None}
    cast_i = [0]

    def emit_casts(n, waits=()):
        for _ in range(n):
            if cast_i[0] >= len(cast_pieces):
                return
            dst, srcp, sem = cast_pieces[cast_i[0]]
            cast_i[0] += 1
            cast_ev[sem] = dma("gpsimd", dst, srcp, waits=waits, sem=sem,
                               serial=False)

    ev_ini = None
    for dst, src in [(ident_sb[:, :], ident_d[:, :]),
                     (onescol_sb[:, :], onescol_d[:, :]),
                     (onesr_sb[:, :], onesr_d[:, :]),
                     (ones128_sb[:, :], ones128_d[:, :]),
                     (b1r_sb[:, :], b1r_d[:, :]),
                     (cos_sb[:, :], cosA_d[:, :]),
                     (sin_sb[:, :], sinA_d[:, :])]:
        ev_ini = dma("sync", dst, src, sem="sini", serial=False)
    ev_eps = dve(lambda e: e.memset(stats[:, 7:8], EPS))
    ev_invd = dve(lambda e: e.memset(stats[:, 15:16], 1.0 / D))

    ldx = {}
    ev_lnap = {}
    ev_red = {}
    ev_half = {}
    ev_P1ev = {}
    ev_qfl = {}
    ev_vc = {}
    ev_kc = {}
    ev_trqk = {}

    for tt in range(min(2, TT)):
        ldx[tt] = dma("sync", x_sb[tt % 2][:, :],
                      x_d[tt * 128:(tt + 1) * 128, :], sem=f"sx{tt % 2}")

    def ln_chain(xap, so, w_extra, out_ap, dump_ap):
        e_sq = act(lambda e: e.activation(dump_ap, xap, AF.Square,
                                          accum_out=stats[:, so + 6:so + 7]),
                   waits=w_extra)
        e_rs = dve(lambda e: e.reduce_sum(stats[:, so + 0:so + 1], xap,
                                          axis=AX.X), waits=w_extra)
        e_mean = dve(lambda e: e.tensor_scalar(
            stats[:, so + 1:so + 2], stats[:, so + 0:so + 1], 1.0 / D, None,
            ALU.mult), waits=_later(e_rs))
        e_msq = dve(lambda e: e.tensor_scalar(
            stats[:, so + 2:so + 3], stats[:, so + 1:so + 2],
            stats[:, so + 1:so + 2], None, ALU.mult), waits=_later(e_mean))
        e_var = dve(lambda e: e.scalar_tensor_tensor(
            stats[:, so + 3:so + 4], stats[:, so + 6:so + 7],
            stats[:, 15:16], stats[:, so + 2:so + 3], ALU.mult,
            ALU.subtract), waits=_later(e_msq, e_sq, ev_invd))
        e_ln = act(lambda e: e.activation(
            stats[:, so + 4:so + 5], stats[:, so + 3:so + 4], AF.Ln,
            bias=stats[:, 7:8]), waits=_later(e_var, ev_eps))
        e_rstd = act(lambda e: e.activation(
            stats[:, so + 4:so + 5], stats[:, so + 4:so + 5], AF.Exp,
            scale=-0.5), waits=_later(e_ln))
        e_nmrs = dve(lambda e: e.tensor_scalar(
            stats[:, so + 5:so + 6], stats[:, so + 1:so + 2],
            stats[:, so + 4:so + 5], -1.0, ALU.mult, ALU.mult),
            waits=_later(e_rstd))
        e_ap = act(lambda e: e.activation(
            out_ap, xap, AF.Identity, bias=stats[:, so + 5:so + 6],
            scale=stats[:, so + 4:so + 5]), waits=_later(e_nmrs))
        return e_ap, e_rs

    ev_eqx = {}
    pend_qktr = [None]
    pend_rope = [None]
    rope_o2 = {}

    def flush_qktr():
        if pend_qktr[0] is not None:
            pend_qktr[0]()
            pend_qktr[0] = None

    def flush_rope():
        if pend_rope[0] is not None:
            pend_rope[0]()
            pend_rope[0] = None

    for tt in range(min(4, TT)):
        ldx[tt] = dma("sync", xA[tt % 4][:, :],
                      x_d[tt * 128:(tt + 1) * 128, :], sem=f"sxa{tt % 4}")
    ev_lntr = {}

    def emit_chain(tt):
        xb = tt % 4
        so = (tt % 4) * 8
        wx = _later(ldx[tt], ev_lnap.get(tt - 4), ev_red.get(tt - 4))
        e_lnb, e_rs = ln_chain(xA[xb][:, :], so, wx, ln_bf[:, :],
                               atmpA[:, :])
        ev_lnap[tt] = e_lnb
        ev_red[tt] = e_rs
        if tt + 4 < TT:
            ldx[tt + 4] = dma("sync", xA[xb][:, :],
                              x_d[(tt + 4) * 128:(tt + 5) * 128, :],
                              waits=_later(e_lnb, e_rs), sem=f"sxa{xb}")
    emit_chain(0)
    for tt in range(TT):
        xb = tt % 4
        qkb = [qk_sb, qk2_sb][tt % 2]
        e_lnb = ev_lnap[tt]
        e_trg = [None, None]
        for g in range(2):
            wts = _later(e_lnb, ev_lntr.get(tt - 1),
                         ev_ini if tt == 0 else None)
            for j in range(8):
                e_trg[g] = tr_ln(
                    TRP[:, g * 1024 + j * 128:g * 1024 + (j + 1) * 128],
                    ln_bf[:, (8 * g + j) * 128:(8 * g + j + 1) * 128],
                    waits=wts, inc="pe" if j == 7 else None)
                wts = ()
        e_ev1 = dve(lambda e: e.tensor_copy(lnT_sb[:, 0:1024],
                                            TRP[:, 0:1024]),
                    waits=_later(e_trg[0]))
        e_ev2 = dve(lambda e: e.tensor_copy(lnT_sb[:, 1024:2048],
                                            TRP[:, 1024:2048]),
                    waits=_later(e_trg[1]))
        ev_lntr[tt] = e_ev2
        if tt + 1 < TT:
            emit_chain(tt + 1)
        flush_rope()
        segw = [512, 512, 512, 512, 256]
        ev_seg = [None] * 5
        for seg in range(5):
            pb = SEG[:, (seg % 2) * 512:(seg % 2) * 512 + segw[seg]]
            guard = {0: ev_half.get((tt - 1, 0)), 1: ev_half.get((tt - 1, 1)),
                     2: ev_half.get((tt, 0)), 3: ev_half.get((tt, 1)),
                     4: ev_half.get((tt, 0))}[seg]
            for ec in range(16):
                wts = _later(e_ev1, e_ev2, ev_wA, guard) if ec == 0 else ()
                ev_seg[seg] = mm(
                    pb, lnT_sb[:, ec * 128:(ec + 1) * 128],
                    wA_sb[:, ec * 2304 + seg * 512:
                          ec * 2304 + seg * 512 + segw[seg]],
                    ec == 0, ec == 15, waits=wts,
                    inc="pe" if ec == 15 else None)
            if seg == 1:
                flush_qktr()
            if seg < 4:
                e_cp = dve(lambda e, seg=seg, qkb=qkb: e.tensor_copy(
                    qkb[:, seg * 512:(seg + 1) * 512],
                    SEG[:, (seg % 2) * 512:(seg % 2) * 512 + 512]),
                    waits=_later(ev_seg[seg], ev_trqk.get(tt - 2)))
                ev_half[(tt, seg % 2)] = e_cp
            else:
                e_cpk = dve(lambda e, qkb=qkb: e.tensor_copy(
                    qkb[:, 2048:2176], SEG[:, 0:128]),
                    waits=_later(ev_seg[4], ev_trqk.get(tt - 2)))
                e_vc = dve(lambda e, tt=tt: e.tensor_copy(
                    V_sb[:, tt * 128:(tt + 1) * 128], SEG[:, 128:256]),
                    waits=_later(ev_seg[4]))
                ev_half[(tt, 0)] = e_vc
                ev_vc[tt] = e_vc
        def rope_emit(tt=tt, qkb=qkb, ev_h0=ev_half[(tt, 0)],
                      ev_h1=ev_half[(tt, 1)]):
            qk3 = qkb[:, :].rearrange("p (h d) -> p h d", d=128)
            A1 = qk3[:, :, 0:64]
            A2 = qk3[:, :, 64:128]
            cosb = cos_sb[:, tt * 64:(tt + 1) * 64].unsqueeze(1) \
                .broadcast_to((128, 17, 64))
            sinb = sin_sb[:, tt * 64:(tt + 1) * 64].unsqueeze(1) \
                .broadcast_to((128, 17, 64))
            rtv = [r[:, :].rearrange("p (h d) -> p h d", d=64) for r in rt]
            e_r1 = dve(lambda e, a=A1, b=cosb: e.tensor_tensor(
                rtv[0], a, b, ALU.mult),
                waits=_later(ev_h0, ev_h1, ev_ini))
            e_r2 = dve(lambda e, a=A2, b=sinb: e.tensor_tensor(
                rtv[1], a, b, ALU.mult))
            e_r3 = dve(lambda e, a=A1, b=sinb: e.tensor_tensor(
                rtv[2], a, b, ALU.mult))
            e_r4 = dve(lambda e, a=A2, b=cosb: e.tensor_tensor(
                rtv[3], a, b, ALU.mult))
            e_o1 = dve(lambda e, a=A1: e.tensor_tensor(
                a, rtv[0], rtv[1], ALU.subtract),
                waits=_later(e_r1, e_r2, e_r3, e_r4))
            e_o2 = dve(lambda e, a=A2: e.tensor_tensor(
                a, rtv[2], rtv[3], ALU.add), waits=_later(e_o1))

            rope_o2[tt] = e_o2

        def qktr(tt=tt, qkb=qkb):
            e_o2 = rope_o2[tt]
            gloc = tt % 4
            half = (tt // 4) % 2
            qgv = qstg[:, half * 8192:(half + 1) * 8192] \
                .rearrange("p (h c) -> p h c", c=512)
            prev = ev_eqx.get(tt - 1)
            gev = [None] * 3
            wts = _later(e_o2, prev[1] if prev else None,
                         prev[2] if prev else None)
            for j in range(8):
                gev[0] = tr_qk(TRP2[:, j * 128:(j + 1) * 128],
                            qkb[:, j * 128:(j + 1) * 128],
                            waits=wts, inc="pe" if j == 7 else None)
                wts = ()
            e_eq0 = dve(lambda e, gloc=gloc, qgv=qgv: e.tensor_copy(
                qgv[:, 0:8, gloc * 128:(gloc + 1) * 128],
                TRP2[:, 0:1024].rearrange("p (h c) -> p h c", c=128)),
                waits=_later(gev[0], ev_qfl.get(tt // 4 - 2)))
            wts = ()
            for j in range(8):
                gev[1] = tr_qk(TRP2[:, 1024 + j * 128:1024 + (j + 1) * 128],
                            qkb[:, (8 + j) * 128:(9 + j) * 128],
                            waits=wts, inc="pe" if j == 7 else None)
            e_eq1 = dve(lambda e, gloc=gloc, qgv=qgv: e.tensor_copy(
                qgv[:, 8:16, gloc * 128:(gloc + 1) * 128],
                TRP2[:, 1024:2048].rearrange("p (h c) -> p h c", c=128)),
                waits=_later(gev[1]))
            gev[2] = tr_qk(TRP2[:, 0:128], qkb[:, 16 * 128:17 * 128],
                        waits=_later(e_eq0), inc="pe")
            e_ek = dve(lambda e, tt=tt: e.tensor_copy(
                K_sb[:, tt * 128:(tt + 1) * 128], TRP2[:, 0:128]),
                waits=_later(gev[2]))
            ev_kc[tt] = e_ek
            ev_trqk[tt] = gev[2]
            ev_eqx[tt] = (e_eq0, e_eq1, e_ek)
            if tt % 4 == 3:
                g4 = tt // 4
                evs = _later(e_eq0, e_eq1, e_ek)
                for h in range(NH):
                    ev_qfl[g4] = dma(
                        "scalar", q_scr[h, :, g4 * 512:(g4 + 1) * 512],
                        qstg[:, half * 8192 + h * 512:
                             half * 8192 + (h + 1) * 512],
                        waits=evs if h == 0 else (), sem="sqf",
                        serial=False)
        pend_rope[0] = rope_emit
        pend_qktr[0] = qktr
        emit_casts(2, waits=_later(ldx[tt]))
    flush_rope()
    flush_qktr()
    emit_casts(1000)
    ev_wo = cast_ev["sgW"]
    ev_p1c = cast_ev["sgP1"]
    ev_p2c = cast_ev["sgP2"]

    PEA = ("pe", S.counts.get("pe", 0))
    DVEA = ("dve", S.counts.get("dve", 0))
    ACTA = ("act", S.counts.get("act", 0))
    SQFA = ("sqf", S.counts.get("sqf", 0))
    G0 = (PEA, DVEA, ACTA, SQFA)

    ev_exp = {}
    ev_sc2 = {}
    ring = {s: [G0] for s in range(8)}
    ev_qsld = {}
    qs_guard = {0: G0, 1: G0}
    ev_avevac = {}
    ev_dmm = {}
    ev_rc = {}
    ev_aT = {}
    ev_mm1 = {}
    ev_gelu = {}
    ev_cp1 = {}
    ev_p1ld = {}
    p1_guard = {0: G0, 1: G0}
    wr_guard = {0: G0, 1: G0}
    wr_i = [0]
    ev_mm2ep = {}
    ev_ysto = {}
    ev_opj = {}
    ev_mm2last = {}
    ev_x1w = {}
    ev_stgdone = {}
    ev_xld = {}
    x_guard = {0: G0, 1: G0}
    ln2slot_guard = {0: G0, 1: G0}
    trc_guard = {0: G0, 1: G0}
    att_tail = {}
    b2c_ev = [None]

    pend_gelu = [None]
    pend_tail = [None, None]

    def flush_gelu():
        if pend_gelu[0] is not None:
            pend_gelu[0]()
            pend_gelu[0] = None

    def flush_tail(i=None):
        for j in ((0, 1) if i is None else (i,)):
            if pend_tail[j] is not None:
                pend_tail[j]()
                pend_tail[j] = None

    def emit_av(c, h, p, s_adds):
        h_g = c * NH + h
        g = h_g * NP + p
        slot = g % 8
        evav = None
        for kk in (2 * p, 2 * p + 1):
            w = ()
            if kk == 2 * p:
                w = _later(ev_exp[g],
                           ev_avevac.get(h_g - 1) if p == 0 else None,
                           G0 if h_g == 0 and p == 0 else None)
            evav = mm_avv(AVp, V_sb[:, kk * 128:(kk + 1) * 128],
                      E_sb[:, slot * 1024 + (kk - 2 * p) * 512:
                           slot * 1024 + (kk - 2 * p + 1) * 512],
                      p == 0 and kk == 2 * p,
                      p == NP - 1 and kk == 2 * p + 1,
                      waits=w, inc="pe")
        ring[slot] = [evav, s_adds[p]]
        return evav

    def emit_head(c, h, take_fill):
        h_g = c * NH + h
        qpar = h % 2
        Sb = SS[h % 2]
        if (c, h) not in ev_qsld:
            ev_qsld[(c, h)] = dma(
                "sync", Qs[qpar][:, :], q_scr[h, :, c * 512:(c + 1) * 512],
                waits=_later(qs_guard[qpar],
                             G0 if c == 0 and h <= 1 else None),
                sem=f"sq{qpar}")
        nx_c, nx_h = (c, h + 1) if h + 1 < NH else (c + 1, 0)
        if nx_c < NCH and (nx_c, nx_h) not in ev_qsld:
            ev_qsld[(nx_c, nx_h)] = dma(
                "sync", Qs[nx_h % 2][:, :],
                q_scr[nx_h, :, nx_c * 512:(nx_c + 1) * 512],
                waits=_later(qs_guard[nx_h % 2],
                             G0 if c == 0 and h == 0 else None),
                sem=f"sq{nx_h % 2}")
        flush_tail(0)
        s_adds = {}
        evav = [None]
        for p in range(NP):
            g = h_g * NP + p
            slot = g % 8
            pp = PP[g % 2]
            w0 = _later(ev_qsld[(c, h)], ev_exp.get(g - 2),
                        ev_stgdone.get(c - 1) if (h == 0 and p == 0)
                        else None,
                        G0 if (c == 0 and h == 0 and p == 0) else None)
            mm_sc(pp[:, 0:512], K_sb[:, (2 * p) * 128:(2 * p + 1) * 128],
                  Qs[qpar][:, :], True, True, waits=w0)
            ev_sc2[g] = mm_sc(pp[:, 512:1024],
                           K_sb[:, (2 * p + 1) * 128:(2 * p + 2) * 128],
                           Qs[qpar][:, :], True, True, inc="pe")
            if p == 2:
                flush_tail(1)
            if p >= 2:
                evav[0] = emit_av(c, h, p - 2, s_adds)
                take_fill(4)
            ev_exp[g] = act(
                lambda e, pp=pp, slot=slot: e.activation(
                    E_sb[:, slot * 1024:(slot + 1) * 1024], pp[:, :],
                    AF.Exp, scale=ISQ),
                waits=_later(ev_sc2[g], *ring[slot]))
            if p == 0:
                s_adds[p] = dve(lambda e, slot=slot, Sb=Sb: e.tensor_tensor(
                    Sb[:, :], E_sb[:, slot * 1024:slot * 1024 + 512],
                    E_sb[:, slot * 1024 + 512:(slot + 1) * 1024], ALU.add),
                    waits=_later(ev_exp[g], ev_dmm.get(h_g - 2)))
            else:
                dve(lambda e, slot=slot: e.tensor_tensor(
                    tmpP[:, :], E_sb[:, slot * 1024:slot * 1024 + 512],
                    E_sb[:, slot * 1024 + 512:(slot + 1) * 1024], ALU.add),
                    waits=_later(ev_exp[g]))
                s_adds[p] = dve(lambda e, Sb=Sb: e.tensor_tensor(
                    Sb[:, :], Sb[:, :], tmpP[:, :], ALU.add))
        evav[0] = emit_av(c, h, NP - 2, s_adds)
        take_fill(4)
        evav[0] = emit_av(c, h, NP - 1, s_adds)
        take_fill(4)
        qs_guard[qpar] = ev_sc2[h_g * NP + NP - 1]

        def tail0(h_g=h_g, evav=evav[0]):
            ev_avevac[h_g] = dve(lambda e: e.tensor_copy(
                avs[:, :], AVp[:, :]), waits=_later(evav))

        def tail1(h_g=h_g, h=h, c=c, Sb=Sb, s_last=s_adds[NP - 1]):
            ev_dmm[h_g] = mm(B7[:, :], ones128_sb[:, :], Sb[:, :],
                             True, True,
                             waits=_later(s_last, ev_rc.get(h_g - 1),
                                          b2c_ev[0] if h == 0 else None,
                                          G0 if h_g == 0 else None),
                             inc="pe")
            ev_rc[h_g] = dve(lambda e: e.reciprocal(rD_sb[:, :], B7[:, :]),
                             waits=_later(ev_dmm[h_g]))
            ev_aT[h_g] = dve(lambda e, h=h: e.tensor_tensor(
                aT_sb[:, h * 512:(h + 1) * 512], avs[:, :], rD_sb[:, :],
                ALU.mult),
                waits=_later(ev_opj.get(c - 1) if h == 0 else None,
                             G0 if h_g == 0 else None))
        pend_tail[0] = tail0
        pend_tail[1] = tail1

    def filler_stream(cf):
        for ht in range(NHT):
            pb = ht % 2
            for ec in range(16):
                def emit_one(ht=ht, ec=ec, pb=pb):
                    if ec == 0:
                        flush_gelu()
                        for hh in (ht, ht + 1):
                            if hh < NHT and (cf, hh) not in ev_p1ld:
                                ev_p1ld[(cf, hh)] = dma(
                                    ("sync", "gpsimd")[hh % 2],
                                    p1s[hh % 2][:, :], p1_bf[hh],
                                    waits=_later(p1_guard[hh % 2],
                                                 ev_p1c if cf == 0 and
                                                 hh <= 1 else None),
                                    sem=f"sp1{hh % 2}")
                        w0 = _later(ev_p1ld[(cf, ht)],
                                    ev_cp1.get((cf, ht - 2)),
                                    ev_stgdone.get(cf),
                                    G0 if cf == 0 and ht <= 1 else None)
                    else:
                        w0 = ()
                    ev = mm_f1(M1[pb], p1s[pb][:, ec * 128:(ec + 1) * 128],
                            stg[:, ec * 512:(ec + 1) * 512],
                            ec == 0, ec == 15, waits=w0,
                            inc="pe" if ec == 15 else None)
                    if ec == 15:
                        ev_mm1[(cf, ht)] = ev
                        p1_guard[pb] = ev
                        ev_cp1[(cf, ht)] = dve(
                            lambda e, ht=ht, pb=pb: e.tensor_copy(
                                hT_sb[:, ht * 512:(ht + 1) * 512], M1[pb]),
                            waits=_later(ev, G0 if cf == 0 else None))
                yield emit_one

    def emit_mm2(cf):
        PSs = [PA, PB]
        tail = att_tail.get(cf + 1, att_tail[cf])
        stop_ev = {}
        for ht in range(NHT):
            ev_gelu[(cf, ht)] = act(
                lambda e, ht=ht: e.activation(
                    hT_sb[:, ht * 512:(ht + 1) * 512],
                    hT_sb[:, ht * 512:(ht + 1) * 512], AF.Gelu,
                    bias=b1r_sb[:, ht:ht + 1]),
                waits=_later(ev_cp1[(cf, ht)], ev_ini))
        W6 = [wr[0], wr[1]] + [stg[:, i * 2048:(i + 1) * 2048]
                               for i in range(4)]
        NW = 6
        ldev = {}
        grp_last = {}

        def issue_load(j):
            hcj = j % NHT
            sl = j % NW
            w = _later(grp_last.get(j - NW),
                       wr_guard[sl] if (j < NW and sl < 2) else None,
                       ev_mm1[(cf, NHT - 1)] if (j < NW and sl >= 2)
                       else None,
                       ev_p2c if cf == 0 and j < NW else None)
            ldev[j] = dma(("sync", "gpsimd", "scalar")[sl % 3],
                          W6[sl][:, :],
                          p2_bf[hcj * 128:(hcj + 1) * 128, :], waits=w,
                          sem=f"swq{sl % 3}")
        for j in range(NW):
            issue_load(j)
        for pr in range(2):
            tls = (2, 3) if pr == 0 else (0, 1)
            if pr == 1:
                for tl in (0, 1):
                    ev_x1r = dma(
                        "gpsimd", x1t[tl % 2][:, :],
                        x1_scr[(cf * 4 + tl) * 128:(cf * 4 + tl + 1) * 128,
                               :],
                        waits=_later(ev_x1w[(cf, tl)],
                                     ev_mm2ep[(cf, tl + 2)]),
                        sem=f"sx1r{tl % 2}")
                    ev_x1w[(cf, tl, "r")] = ev_x1r
            for hc in range(NHT):
                j = pr * NHT + hc
                sl = j % NW
                ev_w = ldev[j]
                last = None
                for ti, tl in enumerate(tls):
                    ps = PSs[ti]
                    for s4 in range(4):
                        w = ()
                        if hc == 0 and s4 == 0:
                            if pr == 0:
                                w = _later(ev_w, ev_gelu[(cf, hc)],
                                           tail[0], tail[1],
                                           ev_stgdone[cf],
                                           ev_cp1[(cf, NHT - 1)]
                                           if ti == 1 else None)
                            else:
                                w = _later(ev_w, ev_mm2ep[(cf, tls[ti] + 2)])
                        elif s4 == 0:
                            w = _later(ev_w, ev_gelu[(cf, hc)])
                        last = mm_f2(ps[:, s4 * 512:(s4 + 1) * 512],
                                     hT_sb[:, hc * 512 + tl * 128:
                                     hc * 512 + (tl + 1) * 128],
                                     W6[sl][:, s4 * 512:(s4 + 1) * 512],
                                     hc == 0, hc == NHT - 1, waits=w,
                                     inc="pe" if (s4 == 3 and
                                                  (ti == 1 or
                                                   hc == NHT - 1))
                                     else None)
                        if s4 == 3 and hc == NHT - 1:
                            stop_ev[tl] = last
                grp_last[j] = last
                if j + NW < 2 * NHT:
                    issue_load(j + NW)
            for ti, tl in enumerate(tls):
                yv = y_stage[:, ti * 2048:(ti + 1) * 2048]
                wts = _later(stop_ev[tl],
                             ev_x1w.get((cf, tl, "r")),
                             ev_ysto.get((cf, tl + 2)),
                             ev_ysto.get((cf - 1, tl - 2)) if pr == 0
                             else None,
                             tail[0] if pr == 0 else None)
                ev_mm2ep[(cf, tl)] = dve(
                    lambda e, yv=yv, ps=PSs[ti], tl=tl: e.tensor_tensor(
                        yv, ps[:, :], x1t[tl % 2][:, :], ALU.add),
                    waits=wts)
                ev_ysto[(cf, tl)] = dma(
                    "scalar",
                    y_d[(cf * 4 + tl) * 128:(cf * 4 + tl + 1) * 128, :],
                    yv, waits=_later(ev_mm2ep[(cf, tl)]), sem="ssy",
                    serial=False)
        wr_guard[0] = grp_last[126]
        wr_guard[1] = grp_last[127]
        ev_mm2last[cf] = stop_ev[1]
        for s in range(8):
            ring[s] = ring[s] + [ev_ysto[(cf, 2 if s < 4 else 3)],
                                 ev_ysto[(cf, 0 if s < 4 else 1)]]

    def emit_outproj(c):
        PSs = [PA, PB]
        if b2c_ev[0] is None:
            e_l = dma("sync", b2b_sb[0:1, :], b2row_d[:, :],
                      waits=_later(G0), sem="sini", serial=False)
            e_c = None
            for s4 in range(4):
                e_m = mm(B7[:, :], onesr_sb[0:1, :],
                         b2b_sb[0:1, s4 * 512:(s4 + 1) * 512], True, True,
                         waits=_later(e_l, ev_aT[c * NH + NH - 1],
                                      e_c), inc="pe")
                e_c = dve(lambda e, s4=s4: e.tensor_copy(
                    b2b_sb[:, s4 * 512:(s4 + 1) * 512], B7[:, :]),
                    waits=_later(e_m))
            b2c_ev[0] = e_c
        for tl in (0, 1):
            ev_xld[(c, tl)] = dma(
                "gpsimd", x_sb[tl % 2][:, :],
                x_d[(c * 4 + tl) * 128:(c * 4 + tl + 1) * 128, :],
                waits=_later(x_guard[tl % 2], G0 if c == 0 else None),
                sem=f"sx{tl % 2}")
        stop_ev = {}
        ev_x1add = {}
        ev_l2ap = {}
        ev_b2ad = {}
        e_evc = {}
        stgv = stg[:, :].rearrange("p (e c) -> p e c", c=512)

        W4 = [wr[0], wr[1], p1s[0], p1s[1],
              E_sb[:, 0:2048], E_sb[:, 2048:4096]]
        W4sem = ["swr0", "swr1", "sp10", "sp11", "swE0", "swE1"]
        NWo = 6
        ldev = {}
        grp_last = {}

        def issue_load(j):
            dhj = j % 16
            sl = j % NWo
            w = _later(grp_last.get(j - NWo),
                       (wr_guard[sl] if sl < 2 else
                        (ev_mm1.get((c - 1, NHT - 1)) if sl < 4 else
                         _later(ev_ysto.get((c - 1, sl - 4)),
                                att_tail[c][0], att_tail[c][1])))
                       if j < NWo else None,
                       G0 if c == 0 and j < NWo else None,
                       ev_wo if c == 0 and j < NWo else None)
            ldev[j] = dma(("sync", "gpsimd", "scalar")[sl % 3], W4[sl][:, :],
                          woT_bf[dhj * 128:(dhj + 1) * 128, :], waits=w,
                          sem=W4sem[sl])
        for j in range(NWo):
            issue_load(j)

        def emit_pr(pr):
            tls = (pr * 2, pr * 2 + 1)
            for dh in range(16):
                j = pr * 16 + dh
                sl = j % NWo
                ev_w = ldev[j]
                last = None
                for ti, tl in enumerate(tls):
                    ps = PSs[ti]
                    for s4 in range(4):
                        w = ()
                        if dh == 0 and s4 == 0:
                            if pr == 0:
                                w = _later(ev_w, ev_aT[c * NH + NH - 1],
                                           att_tail[c][0],
                                           ev_mm2ep.get((c - 1, 1)),
                                           b2c_ev[0])
                            else:
                                w = _later(ev_w, ev_x1add[tls[ti] - 2])
                        elif s4 == 0:
                            w = _later(ev_w)
                        last = mm_op(ps[:, s4 * 512:(s4 + 1) * 512],
                                     aT_sb[:, dh * 512 + tl * 128:
                                     dh * 512 + (tl + 1) * 128],
                                     W4[sl][:, s4 * 512:(s4 + 1) * 512],
                                     dh == 0, dh == 15, waits=w,
                                     inc="pe" if (s4 == 3 and
                                                  (ti == 1 or dh == 15))
                                     else None)
                        if s4 == 3 and dh == 15:
                            stop_ev[tl] = last
                grp_last[j] = last
                if j + NWo < 32:
                    issue_load(j + NWo)
            ev_opj[c] = stop_ev[tls[1]]

        def x1_part(tl):
            xb = tl % 2
            ev_x1add[tl] = dve(
                lambda e, xb=xb, ps=PSs[tl % 2]: e.tensor_tensor(
                    x1t[xb][:, :], ps[:, :], x_sb[xb][:, :], ALU.add),
                waits=_later(stop_ev[tl], ev_xld[(c, tl)],
                             ev_x1w.get((c, tl - 2)),
                             ev_b2ad.get(tl - 2),
                             G0 if c == 0 else None))
            x_guard[xb] = ev_x1add[tl]
            if tl + 2 <= 3:
                ev_xld[(c, tl + 2)] = dma(
                    "gpsimd", x_sb[xb][:, :],
                    x_d[(c * 4 + tl + 2) * 128:(c * 4 + tl + 3) * 128, :],
                    waits=_later(ev_x1add[tl]), sem=f"sx{xb}")

        def ln2_part(tl):
            xb = tl % 2
            so = xb * 8
            wx = _later(ev_x1add[tl], ln2slot_guard[xb],
                        G0 if c == 0 else None)
            e_l2, _ = ln_chain(
                x1t[xb][:, :], so, wx,
                ln2b4[:, xb * 2048:(xb + 1) * 2048],
                ln2b4[:, xb * 2048:(xb + 1) * 2048])
            ev_l2ap[tl] = e_l2
            ev_b2ad[tl] = dve(
                lambda e, xb=xb: e.tensor_tensor(
                    x1t[xb][:, :], x1t[xb][:, :], b2b_sb[:, :], ALU.add),
                waits=_later(e_l2, b2c_ev[0]))
            if tl <= 1:
                ev_x1w[(c, tl)] = dma(
                    "gpsimd",
                    x1_scr[(c * 4 + tl) * 128:(c * 4 + tl + 1) * 128, :],
                    x1t[xb][:, :], waits=_later(ev_b2ad[tl]),
                    sem="sx1w", serial=False)

        def tr_part(tl):
            xb = tl % 2
            buf = TRC[xb]
            wts = _later(ev_l2ap[tl], ev_x1add[2 + xb], trc_guard[xb])
            e_t = None
            for j in range(16):
                e_t = tr(buf[:, j * 128:(j + 1) * 128],
                         ln2b4[:, xb * 2048 + j * 128:
                               xb * 2048 + (j + 1) * 128],
                         waits=wts, inc="pe" if j == 15 else None)
                wts = ()
            ln2slot_guard[xb] = e_t
            e_evc[tl] = dve(lambda e, tl=tl, buf=buf: e.tensor_copy(
                stgv[:, :, tl * 128:(tl + 1) * 128],
                buf[:, :].rearrange("p (e c) -> p e c", c=128)),
                waits=_later(e_t, ev_mm1.get((c - 1, NHT - 1)),
                             ev_mm2last.get(c - 1)))
            trc_guard[xb] = e_evc[tl]

        emit_pr(0)
        x1_part(0)
        x1_part(1)
        ln2_part(0)
        ln2_part(1)
        emit_pr(1)
        x1_part(2)
        x1_part(3)
        tr_part(0)
        tr_part(1)
        ln2_part(2)
        ln2_part(3)
        tr_part(2)
        tr_part(3)
        ev_stgdone[c] = e_evc[3]
        wr_guard[0] = grp_last[30]
        wr_guard[1] = grp_last[31]
        p1_guard[0] = grp_last[26]
        p1_guard[1] = grp_last[27]
        ring[0] = ring[0] + [grp_last[28]]
        ring[1] = ring[1] + [grp_last[28]]
        ring[2] = ring[2] + [grp_last[29]]
        ring[3] = ring[3] + [grp_last[29]]

    for k in range(NCH + 1):
        cA, cF = k, k - 1
        fs = filler_stream(cF) if cF >= 0 else iter(())

        def take_fill(n, fs=fs):
            for _ in range(n):
                try:
                    next(fs)()
                except StopIteration:
                    break
        if cA < NCH:
            for h in range(NH):
                emit_head(cA, h, take_fill)
            flush_tail()
            att_tail[cA] = (ev_exp[(cA * NH + NH - 1) * NP + NP - 1],
                            ev_aT[cA * NH + NH - 1])
        for f in fs:
            f()
        flush_gelu()
        if cF >= 0:
            emit_mm2(cF)
        if cA < NCH:
            emit_outproj(cA)

    dma_sems = [k for k in S.counts if k not in ("pe", "act", "dve")]
    finals = tuple((k, S.counts[k]) for k in dma_sems)
    S.emit("scalar", lambda e: e.nop(),
           waits=tuple((k, v) for k, v in finals
                       if k in ("ssy", "sx1w", "sqf")))
    S.emit("sync", lambda e: e.nop(), waits=finals)

    with nc.Block() as block:
        S.replay(block)
    nc.finalize()
    return nc


def _prep(w_attn, w_out, proj1, proj1_bias, proj2, proj2_bias, T):
    TT = T // 128
    d = {}
    d["wqkvT"] = np.ascontiguousarray(w_attn.T)
    d["woT"] = np.ascontiguousarray(w_out.T)
    d["p1blk"] = np.ascontiguousarray(
        proj1.reshape(NHT, 128, 16, 128).transpose(0, 3, 2, 1)
        .reshape(NHT, 128, 2048))
    d["p2T"] = np.ascontiguousarray(proj2.T)
    d["b1r"] = np.ascontiguousarray(proj1_bias.reshape(NHT, 128).T)
    d["b2row"] = np.ascontiguousarray(proj2_bias.reshape(1, D))
    j = np.arange(0, HD, 2, dtype=np.float64)
    pos = (10000.0 ** ((-2.0 * j - 1.0) / HD)).astype(np.float32)
    ang = np.arange(T, dtype=np.float32)[:, None] * pos[None, :]
    cos = np.cos(ang).astype(np.float32)
    sin = np.sin(ang).astype(np.float32)
    d["cosA"] = np.ascontiguousarray(
        cos.reshape(TT, 128, 64).transpose(1, 0, 2).reshape(128, TT * 64))
    d["sinA"] = np.ascontiguousarray(
        sin.reshape(TT, 128, 64).transpose(1, 0, 2).reshape(128, TT * 64))
    d["ident"] = np.eye(128, dtype=ml_dtypes.bfloat16)
    d["onescol"] = np.ones((128, 1), dtype=np.float32)
    d["ones128"] = np.ones((128, 128), dtype=np.float32)
    d["onesr"] = np.ones((1, 128), dtype=np.float32)
    return d


def kernel(L0_attn_x, L0_attn_w_attn, L0_attn_w_out, L0_ffn_proj1,
           L0_ffn_proj1_bias, L0_ffn_proj2, L0_ffn_proj2_bias,
           L0_ffn_gate_w=None, **_unused):
    x = np.asarray(L0_attn_x, dtype=np.float32)
    B, T, _ = x.shape
    shared = _prep(np.asarray(L0_attn_w_attn, np.float32),
                   np.asarray(L0_attn_w_out, np.float32),
                   np.asarray(L0_ffn_proj1, np.float32),
                   np.asarray(L0_ffn_proj1_bias, np.float32),
                   np.asarray(L0_ffn_proj2, np.float32),
                   np.asarray(L0_ffn_proj2_bias, np.float32), T)
    nc = build_program(T)
    in_maps = []
    for b in range(B):
        m = dict(shared)
        m["x"] = np.ascontiguousarray(x[b])
        in_maps.append(m)
    res = run_bass_kernel_spmd(nc, in_maps, core_ids=list(range(B)))
    out = np.stack([np.asarray(r["y"], dtype=np.float32)
                    for r in res.results], axis=0)
    return out
